# version 11
# baseline (speedup 1.0000x reference)
import numpy as np
import ml_dtypes
from contextlib import ExitStack

import concourse.bass as bass
import concourse.mybir as mybir
from concourse.bass_utils import run_bass_kernel_spmd

F32 = mybir.dt.float32
BF16 = mybir.dt.bfloat16
I8 = mybir.dt.int8
AF = mybir.ActivationFunctionType
ALU = mybir.AluOpType
AXX = mybir.AxisListType.X

B, S, D = 2, 8192, 2048
NH, NKV, HD = 16, 4, 128
INTER = 8192
NM = 16
NTOK_OWN = NM * 128
SCALE = float(1.0 / np.sqrt(128.0))
EPS_LN = 1e-5
EPS_RMS = 1.1920929e-07
BF16_1EM8 = float(np.float32(ml_dtypes.bfloat16(1e-8)))
NEG_BIG = -1e30
A1LVL = 99
BCLVL = 5


def build_program(upto="full", nit=64, nm2=16):
    nc = bass.Bass("TRN2", target_bir_lowering=False)

    xb = nc.dram_tensor("xb", [S, D], F32, kind="ExternalInput").ap()
    x_own = nc.dram_tensor("x_own", [NTOK_OWN, D], F32, kind="ExternalInput").ap()
    w_qkv = nc.dram_tensor("w_qkv", [3072, D], F32, kind="ExternalInput").ap()
    w_proj = nc.dram_tensor("w_proj", [D, D], F32, kind="ExternalInput").ap()
    q_gain = nc.dram_tensor("q_gain", [NH], F32, kind="ExternalInput").ap()
    w_gate = nc.dram_tensor("w_gate", [D, INTER], I8, kind="ExternalInput").ap()
    w_up = nc.dram_tensor("w_up", [D, INTER], I8, kind="ExternalInput").ap()
    w_down = nc.dram_tensor("w_down", [INTER, D], I8, kind="ExternalInput").ap()
    s_gate = nc.dram_tensor("s_gate", [INTER], F32, kind="ExternalInput").ap()
    s_up = nc.dram_tensor("s_up", [INTER], F32, kind="ExternalInput").ap()
    s_down = nc.dram_tensor("s_down", [D], F32, kind="ExternalInput").ap()
    rope_c = nc.dram_tensor("rope_c", [S, 64], F32, kind="ExternalInput").ap()
    rope_s = nc.dram_tensor("rope_s", [S, 64], F32, kind="ExternalInput").ap()
    rope_co = nc.dram_tensor("rope_co", [NTOK_OWN, 64], F32, kind="ExternalInput").ap()
    rope_so = nc.dram_tensor("rope_so", [NTOK_OWN, 64], F32, kind="ExternalInput").ap()
    dmask = nc.dram_tensor("dmask", [128, 512], BF16, kind="ExternalInput").ap()

    out_own = nc.dram_tensor("out_own", [NTOK_OWN, D], F32, kind="ExternalOutput").ap()

    sk = "ExternalOutput" if upto in ("A", "A1") else "Internal"
    scr_K = nc.dram_tensor("scr_K", [NKV, 128, S], BF16, kind=sk).ap()
    scr_V = nc.dram_tensor("scr_V", [S, 512], BF16, kind=sk).ap()
    scr_Q = nc.dram_tensor("scr_Q", [NM, NH, 128, 128], BF16, kind=sk).ap()
    scr_XnT = nc.dram_tensor("scr_XnT", [16, 128, NTOK_OWN], BF16, kind=sk).ap()
    scr_WpT = nc.dram_tensor("scr_WpT", [16, 128, D], BF16, kind=sk).ap()

    root = ExitStack()
    with root:
        sb = lambda name, shape, dt: root.enter_context(nc.sbuf_tensor(name, shape, dt))
        sem = lambda name: root.enter_context(nc.semaphore(name))

        idb = sb("idb", [128, 128], BF16)
        idf = sb("idf", [128, 128], F32)
        eps_ln_t = sb("eps_ln_t", [128, 1], F32)
        eps_rms_t = sb("eps_rms_t", [128, 1], F32)
        gainb = sb("gainb", [128, NH], F32)
        sg_col = sb("sg_col", [128, 64], F32)
        su_col = sb("su_col", [128, 64], F32)
        sd_col = sb("sd_col", [128, 16], F32)
        dmask_t = sb("dmask_t", [128, 512], BF16)

        s_c0 = sem("s_c0")
        s_c1 = sem("s_c1")

        rows_scope = ExitStack()
        sg_row = rows_scope.enter_context(nc.sbuf_tensor("sg_row", [64, 128], F32))
        su_row = rows_scope.enter_context(nc.sbuf_tensor("su_row", [64, 128], F32))
        sd_row = rows_scope.enter_context(nc.sbuf_tensor("sd_row", [16, 128], F32))

        with nc.Block() as blk:
            @blk.sync
            def _(sync):
                sync.dma_start(gainb[:], q_gain.unsqueeze(0).broadcast_to([128, NH])).then_inc(s_c0, 16)
                sync.dma_start(dmask_t[:], dmask).then_inc(s_c0, 16)
                sync.dma_start(sg_row[:], s_gate.rearrange("(a p) -> a p", p=128)).then_inc(s_c0, 16)
                sync.dma_start(su_row[:], s_up.rearrange("(a p) -> a p", p=128)).then_inc(s_c0, 16)
                sync.dma_start(sd_row[:], s_down.rearrange("(a p) -> a p", p=128)).then_inc(s_c0, 16)

            @blk.gpsimd
            def _(g):
                g.memset(idb[:], 0.0)
                g.memset(idf[:], 0.0)
                g.drain()
                g.affine_select(idb[:], idb[:], [[-1, 128]], ALU.not_equal, 1.0,
                                base=0, channel_multiplier=1)
                g.affine_select(idf[:], idf[:], [[-1, 128]], ALU.not_equal, 1.0,
                                base=0, channel_multiplier=1)
                g.memset(eps_ln_t[:], EPS_LN)
                g.memset(eps_rms_t[:], EPS_RMS)

        with ExitStack() as csc:
            pb0 = csc.enter_context(nc.psum_tensor("pb_c0", [128, 512], F32))
            pb1 = csc.enter_context(nc.psum_tensor("pb_c1", [128, 512], F32))
            with nc.Block() as blk:
                @blk.tensor
                def _(t):
                    t.wait_ge(s_c0, 16 * 5)
                    t.transpose(pb0[:, 0:64], sg_row[:], idf[0:64, 0:64]).then_inc(s_c1, 1)
                    t.transpose(pb1[:, 0:64], su_row[:], idf[0:64, 0:64]).then_inc(s_c1, 1)

                @blk.vector
                def _(v):
                    v.wait_ge(s_c0, 16 * 5)
                    v.tensor_scalar(gainb[:], gainb[:], SCALE, None, ALU.mult)
                    v.wait_ge(s_c1, 2)
                    v.tensor_copy(sg_col[:], pb0[:, 0:64])
                    v.tensor_copy(su_col[:], pb1[:, 0:64]).then_inc(s_c1, 1)

            with nc.Block() as blk:
                @blk.tensor
                def _(t):
                    t.transpose(pb0[:, 0:16], sd_row[:], idf[0:16, 0:16]).then_inc(s_c1, 1)

                @blk.vector
                def _(v):
                    v.wait_ge(s_c1, 4)
                    v.tensor_copy(sd_col[:], pb0[:, 0:16])
        rows_scope.close()

        wqkv_scope = ExitStack()
        wqkvT = [wqkv_scope.enter_context(nc.sbuf_tensor(f"wqkvT{g}", [128, 3072], BF16))
                 for g in range(16)]

        s_p0sp = sem("s_p0sp")
        with ExitStack() as p0:
            p0sb = lambda name, shape, dt: p0.enter_context(nc.sbuf_tensor(name, shape, dt))
            wbb = [p0sb(f"p0_wb{i}", [128, 2048], BF16) for i in range(2)]
            gsum = p0sb("p0_gsum", [128, 16], F32)
            scale_bf = p0sb("p0_scalebf", [128, 16], BF16)
            scale_f = p0sb("p0_scalef", [128, 16], F32)
            r0 = p0sb("p0_r0", [128, 16], F32)
            rr = p0sb("p0_r", [128, 16], F32)
            tmp16 = p0sb("p0_tmp16", [128, 16], F32)
            tqt = p0sb("p0_tq", [128, 2048], BF16)
            posb = p0sb("p0_pos", [128, 2048], BF16)
            negb = p0sb("p0_neg", [128, 2048], BF16)
            qd = p0sb("p0_qd", [128, 2048], BF16)
            wtt = p0sb("p0_wt", [128, 2048], BF16)
            wp_stage = p0sb("p0_wpst", [128, 2048], BF16)
            tp0 = [p0.enter_context(nc.psum_tensor(f"p0_tp{i}", [128, 1024], BF16))
                   for i in range(4)]

            s_wld = [sem("s_wld0"), sem("s_wld1")]
            s_tern = sem("s_tern")
            s_p0tp = sem("s_p0tp")
            s_p0ev = sem("s_p0ev")

            NQT, NPT = 24, 16
            NT0 = NQT + NPT

            def wtile_src(i):
                if i < NQT:
                    return w_qkv[128 * i:128 * (i + 1), :]
                j = i - NQT
                return w_proj[128 * j:128 * (j + 1), :]

            with nc.Block() as blk:
                @blk.gpsimd
                def _(g):
                    for i in range(NT0):
                        if i >= 2:
                            g.wait_ge(s_tern, i - 1)
                        g.dma_start(wbb[i % 2][:], wtile_src(i)).then_inc(s_wld[i % 2], 16)

                @blk.vector
                def _(v):
                    for i in range(NT0):
                        v.wait_ge(s_wld[i % 2], 16 * (i // 2 + 1))
                        w = wbb[i % 2]
                        v.tensor_reduce(gsum[:], w[:].rearrange("p (a b) -> p a b", b=128),
                                        axis=AXX, op=ALU.add, apply_absolute_value=True)
                        v.drain()
                        v.tensor_scalar(scale_bf[:], gsum[:], 1.0 / 128.0, None, ALU.mult)
                        v.drain()
                        v.tensor_scalar(scale_bf[:], scale_bf[:], BF16_1EM8, None, ALU.max)
                        v.drain()
                        v.tensor_copy(scale_f[:], scale_bf[:])
                        v.drain()
                        v.reciprocal(r0[:], scale_f[:])
                        v.drain()
                        v.tensor_tensor(tmp16[:], scale_f[:], r0[:], ALU.mult)
                        v.drain()
                        v.tensor_scalar(tmp16[:], tmp16[:], -1.0, 2.0, ALU.mult, ALU.add)
                        v.drain()
                        v.tensor_tensor(rr[:], r0[:], tmp16[:], ALU.mult)
                        v.drain()
                        for g in range(16):
                            v.tensor_scalar(tqt[:, 128 * g:128 * (g + 1)],
                                            w[:, 128 * g:128 * (g + 1)],
                                            rr[:, g:g + 1], None, ALU.mult)
                        v.drain()
                        v.tensor_scalar(posb[:], tqt[:], 0.5, None, ALU.is_gt)
                        v.tensor_scalar(negb[:], tqt[:], -0.5, None, ALU.is_lt)
                        v.drain()
                        v.tensor_tensor(qd[:], posb[:], negb[:], ALU.subtract)
                        v.drain()
                        for g in range(15):
                            v.tensor_scalar(wtt[:, 128 * g:128 * (g + 1)],
                                            qd[:, 128 * g:128 * (g + 1)],
                                            scale_f[:, g:g + 1], None, ALU.mult)
                        v.tensor_scalar(wtt[:, 1920:2048], qd[:, 1920:2048],
                                        scale_f[:, 15:16], None, ALU.mult).then_inc(s_tern, 1)
                        if i > NQT:
                            v.wait_ge(s_p0sp, 16 * (i - NQT))
                        for g in range(16):
                            n = 16 * i + g + 1
                            v.wait_ge(s_p0tp, n)
                            src = tp0[(n - 1) % 4][:, 0:128]
                            if i < NQT:
                                v.tensor_copy(wqkvT[g][:, 128 * i:128 * (i + 1)], src).then_inc(s_p0ev, 1)
                            else:
                                v.tensor_copy(wp_stage[:, 128 * g:128 * (g + 1)], src).then_inc(s_p0ev, 1)

                @blk.tensor
                def _(t):
                    for i in range(NT0):
                        t.wait_ge(s_tern, i + 1)
                        for g in range(16):
                            n = 16 * i + g + 1
                            if n > 4:
                                t.wait_ge(s_p0ev, n - 4)
                            t.transpose(tp0[(n - 1) % 4][:, 0:128],
                                        wtt[:, 128 * g:128 * (g + 1)], idb[:]).then_inc(s_p0tp, 1)

                @blk.sync
                def _(sync):
                    for j in range(NPT):
                        i = NQT + j
                        sync.wait_ge(s_p0ev, 16 * (i + 1))
                        dst = scr_WpT[:, :, 128 * j:128 * (j + 1)].rearrange("c h t -> h c t")
                        sync.dma_start(dst, wp_stage[:].rearrange("h (c t) -> h c t", t=128)
                                       ).then_inc(s_p0sp, 16)

        if upto == "0":
            dbg = nc.dram_tensor("dbg_wqkvT", [16, 128, 3072], BF16, kind="ExternalOutput").ap()
            s_dbg = sem("s_dbg")
            with nc.Block() as blk:
                @blk.sync
                def _(sync):
                    for g in range(16):
                        sync.dma_start(dbg[g], wqkvT[g][:]).then_inc(s_dbg, 16)
                    sync.wait_ge(s_dbg, 256)
                    sync.wait_ge(s_p0sp, 16 * 16)
            wqkv_scope.close()
            return nc

        with nc.Block() as blk:
            @blk.sync
            def _(sync):
                sync.wait_ge(s_p0sp, 16 * 16)

        with ExitStack() as pa:
            asb = lambda name, shape, dt: pa.enter_context(nc.sbuf_tensor(name, shape, dt))
            xbuf = [asb(f"a_x{i}", [128, 2048], F32) for i in range(2)]
            xsum = asb("a_xsum", [128, 1], F32)
            xsq = asb("a_xsq", [128, 1], F32)
            meant = asb("a_mean", [128, 1], F32)
            m2t = asb("a_m2", [128, 1], F32)
            vart = asb("a_var", [128, 1], F32)
            rstd = asb("a_rstd", [128, 1], F32)
            nbias = asb("a_nbias", [128, 1], F32)
            xn = asb("a_xn", [128, 2048], BF16)
            xnT = asb("a_xnT", [128, 2048], BF16)
            ksq = asb("a_ksq", [128, 4], F32)
            sq_scr = asb("a_sqscr", [128, 512], F32)
            krstd = asb("a_krstd", [128, 4], F32)
            kn = asb("a_kn", [128, 512], F32)
            rt1 = asb("a_rt1", [128, 2048], F32)
            rt2 = asb("a_rt2", [128, 2048], F32)
            kro = asb("a_kro", [128, 512], BF16)
            ktst = asb("a_ktst", [128, 2048], BF16)
            vstage = [asb(f"a_vst{i}", [128, 512], BF16) for i in range(2)]
            costab = asb("a_cos", [128, 4096], F32)
            sintab = asb("a_sin", [128, 4096], F32)
            qcostab = asb("a_qcos", [128, 1024], F32)
            qsintab = asb("a_qsin", [128, 1024], F32)
            qsq = asb("a_qsq", [128, 16], F32)
            qrstd = asb("a_qrstd", [128, 16], F32)
            qn = asb("a_qn", [128, 2048], F32)
            qro = asb("a_qro", [128, 2048], BF16)
            qtst = asb("a_qtst", [128, 2048], BF16)

            ps_k = pa.enter_context(nc.psum_tensor("a_psk", [128, 512], F32))
            ps_v = pa.enter_context(nc.psum_tensor("a_psv", [128, 512], F32))
            tp = [pa.enter_context(nc.psum_tensor(f"a_tp{i}", [128, 1024], BF16))
                  for i in range(4)]
            ps_q = [ps_k, ps_v] + [pa.enter_context(nc.psum_tensor(f"a_psq{i}", [128, 512], F32))
                                   for i in range(2)]

            s_ax = [sem("s_ax0"), sem("s_ax1")]
            s_tab = sem("s_tab")
            s_var = sem("s_var")
            s_rstd = sem("s_rstd")
            s_nb = sem("s_nb")
            s_xn = sem("s_xn")
            s_tp = sem("s_tp")
            s_ev = sem("s_ev")
            s_mmk = sem("s_mmk")
            s_krstd = sem("s_krstd")
            s_kn = sem("s_kn")
            s_kro = sem("s_kro")
            s_vc = sem("s_vc")
            s_ksp = sem("s_ksp")
            s_vsp = [sem("s_vsp0"), sem("s_vsp1")]

            NIT = nit
            TABS = 64

            with nc.Block() as blk:
                @blk.sync
                def _(sync):
                    sync.dma_start(costab[:], rope_c.rearrange("(a p) d -> p a d", p=128)).then_inc(s_tab, 16)
                    sync.dma_start(sintab[:], rope_s.rearrange("(a p) d -> p a d", p=128)).then_inc(s_tab, 16)
                    sync.dma_start(qcostab[:], rope_co.rearrange("(a p) d -> p a d", p=128)).then_inc(s_tab, 16)
                    sync.dma_start(qsintab[:], rope_so.rearrange("(a p) d -> p a d", p=128)).then_inc(s_tab, 16)
                    for i in range(NIT):
                        if i >= 2:
                            sync.wait_ge(s_xn if A1LVL >= 2 else s_rstd, i - 1)
                        sync.dma_start(xbuf[i % 2][:], xb[128 * i:128 * (i + 1), :]).then_inc(s_ax[i % 2], 16)

                @blk.gpsimd
                def _(g):
                    for i in range(NIT):
                        if A1LVL >= 3:
                            g.wait_ge(s_vc, i + 1)
                            g.dma_start(scr_V[128 * i:128 * (i + 1), :], vstage[i % 2][:]).then_inc(s_vsp[i % 2], 16)
                        if A1LVL >= 4 and i % 4 == 3:
                            g.wait_ge(s_ev, 20 * (i + 1))
                            for hd in range(4):
                                g.dma_start(scr_K[hd][:, 128 * (i - 3):128 * (i + 1)],
                                            ktst[:, 512 * hd:512 * (hd + 1)]).then_inc(s_ksp, 16)

                @blk.vector
                def _(v):
                    for i in range(NIT):
                        if i == 0:
                            v.wait_ge(s_tab, 64)
                        v.wait_ge(s_ax[i % 2], 16 * (i // 2 + 1))
                        if i > 0:
                            v.wait_ge(s_rstd, i)
                        x = xbuf[i % 2]
                        v.tensor_reduce(xsum[:], x[:], axis=AXX, op=ALU.add)
                        v.scalar_tensor_tensor(rt1[:], x[:], 1.0, x[:], ALU.mult, ALU.mult,
                                               accum_out=xsq[:])
                        v.drain()
                        v.tensor_scalar(meant[:], xsum[:], 1.0 / 2048.0, None, ALU.mult)
                        v.drain()
                        v.tensor_tensor(m2t[:], meant[:], meant[:], ALU.mult)
                        v.drain()
                        v.scalar_tensor_tensor(vart[:], xsq[:], 1.0 / 2048.0, m2t[:],
                                               ALU.mult, ALU.subtract)
                        v.drain()
                        v.nop().then_inc(s_var, 1)
                        v.wait_ge(s_rstd, i + 1)
                        v.tensor_scalar(nbias[:], meant[:], rstd[:], -1.0, ALU.mult, ALU.mult).then_inc(s_nb, 1)
                        if A1LVL < 2:
                            continue
                        if i > 0 and A1LVL >= 3:
                            v.wait_ge(s_mmk, 2 * i)
                        for g in range(16):
                            n = 20 * i + g + 1
                            v.wait_ge(s_tp, n)
                            v.tensor_copy(xnT[:, 128 * g:128 * (g + 1)],
                                          tp[(n - 1) % 4][:, 0:128]).then_inc(s_ev, 1)
                        if A1LVL < 4:
                            v.nop().then_inc(s_ev, 4)
                            if A1LVL >= 3:
                                if i > 1:
                                    v.wait_ge(s_vsp[i % 2], 16 * (i // 2))
                                v.wait_ge(s_mmk, 2 * i + 2)
                                v.tensor_copy(vstage[i % 2][:], ps_v[:]).then_inc(s_vc, 1)
                            continue
                        v.wait_ge(s_krstd, i + 1)
                        for seg in range(4):
                            v.tensor_scalar(kn[:, 128 * seg:128 * (seg + 1)],
                                            ps_k[:, 128 * seg:128 * (seg + 1)],
                                            krstd[:, seg:seg + 1], None, ALU.mult)
                        v.drain()
                        v.nop().then_inc(s_kn, 1)
                        if i > 0:
                            v.wait_ge(s_tp, 20 * i)
                        kn3 = kn[:].rearrange("p (h d) -> p h d", d=128)
                        kro3 = kro[:].rearrange("p (h d) -> p h d", d=128)
                        csl = costab[:, 64 * i:64 * (i + 1)].unsqueeze(1).broadcast_to([128, 4, 64])
                        ssl = sintab[:, 64 * i:64 * (i + 1)].unsqueeze(1).broadcast_to([128, 4, 64])
                        r13 = rt1[:, 0:256].rearrange("p (h d) -> p h d", d=64)
                        r23 = rt2[:, 0:256].rearrange("p (h d) -> p h d", d=64)
                        v.tensor_tensor(r13, kn3[:, :, 0:64], csl, ALU.mult)
                        v.tensor_tensor(r23, kn3[:, :, 64:128], ssl, ALU.mult)
                        v.drain()
                        r13b = rt1[:, 256:512].rearrange("p (h d) -> p h d", d=64)
                        r23b = rt2[:, 256:512].rearrange("p (h d) -> p h d", d=64)
                        v.tensor_tensor(kro3[:, :, 0:64], r13, r23, ALU.add)
                        v.tensor_tensor(r13b, kn3[:, :, 64:128], csl, ALU.mult)
                        v.tensor_tensor(r23b, kn3[:, :, 0:64], ssl, ALU.mult)
                        v.drain()
                        v.tensor_tensor(kro3[:, :, 64:128], r13b, r23b, ALU.subtract).then_inc(s_kro, 1)
                        if i % 4 == 0 and i > 0:
                            v.wait_ge(s_ksp, 64 * (i // 4))
                        for g in range(4):
                            n = 20 * i + 16 + g + 1
                            v.wait_ge(s_tp, n)
                            col = 512 * g + 128 * (i % 4)
                            v.tensor_copy(ktst[:, col:col + 128],
                                          tp[(n - 1) % 4][:, 0:128]).then_inc(s_ev, 1)
                        if i > 1:
                            v.wait_ge(s_vsp[i % 2], 16 * (i // 2))
                        v.wait_ge(s_mmk, 2 * i + 2)
                        v.tensor_copy(vstage[i % 2][:], ps_v[:]).then_inc(s_vc, 1)

                @blk.scalar
                def _(s):
                    for i in range(NIT):
                        s.wait_ge(s_var, i + 1)
                        s.activation(rstd[:], vart[:], AF.Ln, bias=eps_ln_t[:], scale=1.0)
                        s.drain()
                        s.activation(rstd[:], rstd[:], AF.Exp, scale=-0.5)
                        s.drain()
                        s.nop().then_inc(s_rstd, 1)
                        if A1LVL < 2:
                            continue
                        s.wait_ge(s_nb, i + 1)
                        if i > 0:
                            s.wait_ge(s_tp, 20 * (i - 1) + 16)
                        s.activation(xn[:], xbuf[i % 2][:], AF.Identity, bias=nbias[:],
                                     scale=rstd[:]).then_inc(s_xn, 1)
                        if A1LVL < 4:
                            continue
                        s.wait_ge(s_mmk, 2 * i + 1)
                        for seg in range(4):
                            s.activation(sq_scr[:, 128 * seg:128 * (seg + 1)],
                                         ps_k[:, 128 * seg:128 * (seg + 1)],
                                         AF.Square, accum_out=ksq[:, seg:seg + 1])
                        s.drain()
                        s.activation(krstd[:], ksq[:], AF.Ln, bias=eps_rms_t[:], scale=1.0 / 128.0)
                        s.drain()
                        s.activation(krstd[:], krstd[:], AF.Exp, scale=-0.5)
                        s.drain()
                        s.nop().then_inc(s_krstd, 1)

                @blk.tensor
                def _(t):
                    for i in range(NIT):
                        if A1LVL < 2:
                            continue
                        t.wait_ge(s_xn, i + 1)
                        for g in range(16):
                            n = 20 * i + g + 1
                            if n > 4:
                                t.wait_ge(s_ev, n - 4)
                            t.transpose(tp[(n - 1) % 4][:, 0:128],
                                        xn[:, 128 * g:128 * (g + 1)], idb[:]).then_inc(s_tp, 1)
                        if A1LVL < 3:
                            t.nop().then_inc(s_tp, 4)
                            continue
                        t.wait_ge(s_ev, 20 * i + 16)
                        if i > 0 and A1LVL >= 4:
                            t.wait_ge(s_kn, i)
                        for h in range(15):
                            t.matmul(ps_k[:], xnT[:, 128 * h:128 * (h + 1)],
                                     wqkvT[h][:, 2048:2560],
                                     start=(h == 0), stop=False)
                        t.matmul(ps_k[:], xnT[:, 1920:2048], wqkvT[15][:, 2048:2560],
                                 start=False, stop=True).then_inc(s_mmk, 1)
                        if i > 0:
                            t.wait_ge(s_vc, i)
                        for h in range(15):
                            t.matmul(ps_v[:], xnT[:, 128 * h:128 * (h + 1)],
                                     wqkvT[h][:, 2560:3072],
                                     start=(h == 0), stop=False)
                        t.matmul(ps_v[:], xnT[:, 1920:2048], wqkvT[15][:, 2560:3072],
                                 start=False, stop=True).then_inc(s_mmk, 1)
                        if A1LVL < 4:
                            t.nop().then_inc(s_tp, 4)
                            continue
                        t.wait_ge(s_kro, i + 1)
                        for g in range(4):
                            n = 20 * i + 16 + g + 1
                            if n > 4:
                                t.wait_ge(s_ev, n - 4)
                            t.transpose(tp[(n - 1) % 4][:, 0:128],
                                        kro[:, 128 * g:128 * (g + 1)], idb[:]).then_inc(s_tp, 1)

            with nc.Block() as blk:
                @blk.sync
                def _(sync):
                    if A1LVL >= 3:
                        sync.wait_ge(s_vsp[0], 16 * (NIT // 2))
                        sync.wait_ge(s_vsp[1], 16 * (NIT // 2))
                    if A1LVL >= 4:
                        sync.wait_ge(s_ksp, 64 * (NIT // 4))
                    if upto == "A1":
                        sync.wait_ge(s_p0sp, 16 * NPT)
            emit_a2 = (upto != "A1")
            s_ax2 = [sem("s_ax20"), sem("s_ax21")]
            s_var2 = sem("s_var2")
            s_rstd2 = sem("s_rstd2")
            s_nb2 = sem("s_nb2")
            s_xn2 = sem("s_xn2")
            s_tp2 = sem("s_tp2")
            s_ev2 = sem("s_ev2")
            s_mm2 = sem("s_mm2")
            s_qrstd = sem("s_qrstd")
            s_qn = sem("s_qn")
            s_qro = sem("s_qro")
            s_qsp = sem("s_qsp")
            s_xsp = sem("s_xsp")

            NM2 = nm2 if emit_a2 else 0
            with nc.Block() as blk:
                @blk.sync
                def _(sync):
                    for m in range(NM2):
                        if m >= 2:
                            sync.wait_ge(s_xn2, m - 1)
                        sync.dma_start(xbuf[m % 2][:], x_own[128 * m:128 * (m + 1), :]).then_inc(s_ax2[m % 2], 16)

                @blk.gpsimd
                def _(g):
                    for m in range(NM2):
                        g.wait_ge(s_ev2, 32 * m + 16)
                        dst = scr_XnT[:, :, 128 * m:128 * (m + 1)].rearrange("c h t -> h c t")
                        g.dma_start(dst, xnT[:].rearrange("h (c t) -> h c t", t=128)).then_inc(s_xsp, 16)
                        g.wait_ge(s_ev2, 32 * (m + 1))
                        dstq = scr_Q[m].rearrange("h d q -> d h q")
                        g.dma_start(dstq, qtst[:].rearrange("d (h q) -> d h q", q=128)).then_inc(s_qsp, 16)

                @blk.vector
                def _(v):
                    for m in range(NM2):
                        v.wait_ge(s_ax2[m % 2], 16 * (m // 2 + 1))
                        if m > 0:
                            v.wait_ge(s_rstd2, m)
                        x = xbuf[m % 2]
                        v.tensor_reduce(xsum[:], x[:], axis=AXX, op=ALU.add)
                        v.scalar_tensor_tensor(rt1[:], x[:], 1.0, x[:], ALU.mult, ALU.mult,
                                               accum_out=xsq[:])
                        v.drain()
                        v.tensor_scalar(meant[:], xsum[:], 1.0 / 2048.0, None, ALU.mult)
                        v.drain()
                        v.tensor_tensor(m2t[:], meant[:], meant[:], ALU.mult)
                        v.drain()
                        v.scalar_tensor_tensor(vart[:], xsq[:], 1.0 / 2048.0, m2t[:],
                                               ALU.mult, ALU.subtract)
                        v.drain()
                        v.nop().then_inc(s_var2, 1)
                        v.wait_ge(s_rstd2, m + 1)
                        v.tensor_scalar(nbias[:], meant[:], rstd[:], -1.0, ALU.mult, ALU.mult).then_inc(s_nb2, 1)
                        if m > 0:
                            v.wait_ge(s_xsp, 16 * m)
                            v.wait_ge(s_mm2, 4 * m)
                        for g in range(16):
                            n = 32 * m + g + 1
                            v.wait_ge(s_tp2, n)
                            v.tensor_copy(xnT[:, 128 * g:128 * (g + 1)],
                                          tp[(n - 1) % 4][:, 0:128]).then_inc(s_ev2, 1)
                        v.wait_ge(s_qrstd, m + 1)
                        for seg in range(16):
                            v.tensor_scalar(qn[:, 128 * seg:128 * (seg + 1)],
                                            ps_q[seg // 4][:, 128 * (seg % 4):128 * (seg % 4) + 128],
                                            qrstd[:, seg:seg + 1],
                                            gainb[:, seg:seg + 1], ALU.mult, ALU.mult)
                        v.drain()
                        v.nop().then_inc(s_qn, 4)
                        if m > 0:
                            v.wait_ge(s_tp2, 32 * m)
                        qn3 = qn[:].rearrange("p (h d) -> p h d", d=128)
                        qro3 = qro[:].rearrange("p (h d) -> p h d", d=128)
                        csl = qcostab[:, 64 * m:64 * (m + 1)].unsqueeze(1).broadcast_to([128, 16, 64])
                        ssl = qsintab[:, 64 * m:64 * (m + 1)].unsqueeze(1).broadcast_to([128, 16, 64])
                        r13 = rt1[:, 0:1024].rearrange("p (h d) -> p h d", d=64)
                        r23 = rt2[:, 0:1024].rearrange("p (h d) -> p h d", d=64)
                        r13b = rt1[:, 1024:2048].rearrange("p (h d) -> p h d", d=64)
                        r23b = rt2[:, 1024:2048].rearrange("p (h d) -> p h d", d=64)
                        v.tensor_tensor(r13, qn3[:, :, 0:64], csl, ALU.mult)
                        v.tensor_tensor(r23, qn3[:, :, 64:128], ssl, ALU.mult)
                        v.drain()
                        v.tensor_tensor(qro3[:, :, 0:64], r13, r23, ALU.add)
                        v.tensor_tensor(r13b, qn3[:, :, 64:128], csl, ALU.mult)
                        v.tensor_tensor(r23b, qn3[:, :, 0:64], ssl, ALU.mult)
                        v.drain()
                        v.tensor_tensor(qro3[:, :, 64:128], r13b, r23b, ALU.subtract).then_inc(s_qro, 1)
                        if m > 0:
                            v.wait_ge(s_qsp, 16 * m)
                        for g in range(16):
                            n = 32 * m + 16 + g + 1
                            v.wait_ge(s_tp2, n)
                            v.tensor_copy(qtst[:, 128 * g:128 * (g + 1)],
                                          tp[(n - 1) % 4][:, 0:128]).then_inc(s_ev2, 1)

                @blk.scalar
                def _(s):
                    for m in range(NM2):
                        s.wait_ge(s_var2, m + 1)
                        s.activation(rstd[:], vart[:], AF.Ln, bias=eps_ln_t[:], scale=1.0)
                        s.drain()
                        s.activation(rstd[:], rstd[:], AF.Exp, scale=-0.5)
                        s.drain()
                        s.nop().then_inc(s_rstd2, 1)
                        s.wait_ge(s_nb2, m + 1)
                        if m > 0:
                            s.wait_ge(s_tp2, 32 * (m - 1) + 16)
                        s.activation(xn[:], xbuf[m % 2][:], AF.Identity, bias=nbias[:],
                                     scale=rstd[:]).then_inc(s_xn2, 1)
                        s.wait_ge(s_mm2, 4 * (m + 1))
                        for seg in range(16):
                            if seg > 0 and seg % 4 == 0:
                                s.drain()
                            s.activation(sq_scr[:, 128 * (seg % 4):128 * (seg % 4) + 128],
                                         ps_q[seg // 4][:, 128 * (seg % 4):128 * (seg % 4) + 128],
                                         AF.Square, accum_out=qsq[:, seg:seg + 1])
                        s.drain()
                        s.activation(qrstd[:], qsq[:], AF.Ln, bias=eps_rms_t[:], scale=1.0 / 128.0)
                        s.drain()
                        s.activation(qrstd[:], qrstd[:], AF.Exp, scale=-0.5)
                        s.drain()
                        s.nop().then_inc(s_qrstd, 1)

                @blk.tensor
                def _(t):
                    for m in range(NM2):
                        t.wait_ge(s_xn2, m + 1)
                        for g in range(16):
                            n = 32 * m + g + 1
                            if n > 4:
                                t.wait_ge(s_ev2, n - 4)
                            t.transpose(tp[(n - 1) % 4][:, 0:128],
                                        xn[:, 128 * g:128 * (g + 1)], idb[:]).then_inc(s_tp2, 1)
                        t.wait_ge(s_ev2, 32 * m + 16)
                        if m > 0:
                            t.wait_ge(s_qn, 4 * m)
                        for oc in range(4):
                            for h in range(15):
                                t.matmul(ps_q[oc][:], xnT[:, 128 * h:128 * (h + 1)],
                                         wqkvT[h][:, 512 * oc:512 * (oc + 1)],
                                         start=(h == 0), stop=False)
                            t.matmul(ps_q[oc][:], xnT[:, 1920:2048],
                                     wqkvT[15][:, 512 * oc:512 * (oc + 1)],
                                     start=False, stop=True).then_inc(s_mm2, 1)
                        t.wait_ge(s_qro, m + 1)
                        for g in range(16):
                            n = 32 * m + 16 + g + 1
                            if n > 4:
                                t.wait_ge(s_ev2, n - 4)
                            t.transpose(tp[(n - 1) % 4][:, 0:128],
                                        qro[:, 128 * g:128 * (g + 1)], idb[:]).then_inc(s_tp2, 1)

            with nc.Block() as blk:
                @blk.sync
                def _(sync):
                    sync.wait_ge(s_qsp, 16 * NM2)
                    sync.wait_ge(s_xsp, 16 * NM2)
                    sync.wait_ge(s_p0sp, 16 * NPT)

        wqkv_scope.close()

        if upto in ("A", "A1"):
            return nc

        _emit_phase_bc(nc, root, sem, dict(
            scr_K=scr_K, scr_V=scr_V, scr_Q=scr_Q, scr_XnT=scr_XnT, scr_WpT=scr_WpT,
            w_gate=w_gate, w_up=w_up, w_down=w_down, x_own=x_own, out_own=out_own,
            idb=idb, idf=idf, eps_rms_t=eps_rms_t, dmask_t=dmask_t,
            sg_col=sg_col, su_col=su_col, sd_col=sd_col))
        return nc


def _emit_phase_bc(nc, root, sem, env):
    scr_K, scr_V, scr_Q = env["scr_K"], env["scr_V"], env["scr_Q"]
    scr_XnT, scr_WpT = env["scr_XnT"], env["scr_WpT"]
    w_gate, w_up, w_down = env["w_gate"], env["w_up"], env["w_down"]
    x_own, out_own = env["x_own"], env["out_own"]
    idb, idf = env["idb"], env["idf"]
    eps_rms_t, dmask_t = env["eps_rms_t"], env["dmask_t"]
    sg_col, su_col, sd_col = env["sg_col"], env["su_col"], env["sd_col"]

    pb = ExitStack()
    with pb:
        bsb = lambda name, shape, dt: pb.enter_context(nc.sbuf_tensor(name, shape, dt))
        PT = bsb("b_PT", [128, 32768], BF16)
        scrF = bsb("b_S", [128, 8192], F32)
        Pb = bsb("b_P", [128, 8192], BF16)
        shB = bsb("b_shB", [128, 16384], BF16)
        shC = bsb("b_shC", [128, 16384], BF16)
        xnT_o = bsb("b_xnT", [128, 8192], BF16)
        qsb2 = bsb("b_q2", [128, 1024], BF16)
        qsb = [qsb2[:, 0:512], qsb2[:, 512:1024]]
        sT_hc = qsb2[:].bitcast(F32)
        wpof_b = None
        gTsil = bsb("b_gts", [128, 512], F32)
        sil = gTsil
        gT = gTsil
        ystage = bsb("b_yst", [128, 512], BF16)
        diagb = None
        xres_b = bsb("b_xr0", [128, 512], F32)
        xres = [xres_b, xres_b]
        ostage = bsb("b_ost", [128, 512], F32)
        ncm = bsb("b_ncm", [128, 32], F32)
        dens = bsb("b_dens", [128, 32], F32)
        dlt = bsb("b_dlt", [128, 32], F32)
        edt = bsb("b_edt", [128, 32], F32)
        sct = bsb("b_sct", [128, 32], F32)
        nm_fin = bsb("b_nmf", [128, 1], F32)
        den_tot = bsb("b_dent", [128, 1], F32)
        rden = bsb("b_rden", [128, 1], F32)
        ssq = bsb("b_ssq", [128, 1], F32)
        ssq_p = bsb("b_ssqp", [128, 4], F32)
        yrstd = bsb("b_yrstd", [128, 1], F32)
        rms_scr = ystage

        ps_s = [pb.enter_context(nc.psum_tensor(f"b_pss{i}", [128, 512], F32))
                for i in range(3)]
        tpg = [pb.enter_context(nc.psum_tensor(f"b_tp{i}", [128, 2048], BF16))
               for i in range(2)]
        ps_acc = pb.enter_context(nc.psum_tensor("b_acc", [128, 512], F32))

        names = ["b_qld", "b_kv", "b_smm", "b_pexp", "b_cm", "b_sm1", "b_ed", "b_pn", "b_dg",
                 "b_ptp", "b_ptev", "b_pv", "b_yev", "b_rsq", "b_yrstd", "b_yrs",
                 "b_wg", "b_gum", "b_sil", "b_fu", "b_wp", "b_wd", "b_xr",
                 "b_pmm", "b_sev", "b_dmm", "b_gsil", "b_z", "b_out", "b_pz"]
        SM = {n: sem(n) for n in names}
        (b_qld, b_kv, b_smm, b_pexp, b_cm, b_sm1, b_ed, b_pn, b_dg, b_ptp, b_ptev, b_pv,
         b_yev, b_rsq, b_yrstd, b_yrs, b_wg, b_gum, b_sil, b_fu, b_wp, b_wd,
         b_xr, b_pmm, b_sev, b_dmm, b_gsil, b_z, b_out, b_pz) = (SM[n] for n in names)
        b_qld2 = [sem("b_qld0"), sem("b_qld1")]
        b_wg2 = [sem("b_wg20"), sem("b_wg21")]
        b_wd2 = [sem("b_wd20"), sem("b_wd21")]
        b_xnt = sem("b_xnt")

        wgbuf = [shB[:, 0:8192], shC[:, 8192:16384]]
        wubuf = [shC[:, 0:8192], scrF[:].bitcast(BF16)[:, 0:8192]]
        wdbuf = [scrF[:].bitcast(BF16)[:, 8192:16384], Pb[:, 0:8192]]
        wpof_b = scrF[:].bitcast(BF16)[:, 0:2048]
        diagb = scrF[:].bitcast(BF16)[:, 0:4096]
        Ksb = shC[:, 0:8192]
        Vtile = lambda ks: shC[:, 8192 + 128 * ks: 8192 + 128 * (ks + 1)]
        PTtile = lambda ks: PT[:, 512 * ks:512 * (ks + 1)]
        PT3 = PT[:].rearrange("p (n q) -> p n q", q=512)
        fuseTt = lambda osub: PT[:, 512 * osub:512 * (osub + 1)]
        ybufq = lambda qs: shB[:, 2048 * qs:2048 * (qs + 1)]
        yb3 = shB[:, 0:8192].rearrange("p (q c) -> p q c", q=4)
        yrTt = lambda c: shB[:, 8192 + 512 * c:8192 + 512 * (c + 1)]
        yr3 = shB[:, 8192:16384].rearrange("p (c t) -> p c t", c=16)
        xn3 = lambda c: xnT_o[:, 512 * c:512 * (c + 1)]
        ost3 = ostage[:].rearrange("p (a h) -> p a h", a=4)

        def tp_slot(idx):
            return tpg[(idx // 2) % 2][:, 1024 * (idx % 2):1024 * (idx % 2) + 128]

        def tp_slot_f32(idx):
            return tpg[(idx // 2) % 2][:].bitcast(F32)[:, 512 * (idx % 2):512 * (idx % 2) + 128]

        def tp_pair(idx):
            return tpg[(idx // 2) % 2][:].rearrange("p (s c) -> p s c", s=2)[:, :, 0:128]

        def tp_pair_f32(idx):
            return tpg[(idx // 2) % 2][:].bitcast(F32).rearrange("p (s c) -> p s c", s=2)[:, :, 0:128]

        heads = [(t, g, hq) for t in range(4) for g in range(4) for hq in range(4)]
        nchunk = lambda t, qs: 4 * t + qs + 1

        smm_at = {}
        acc = 0
        for hidx, (t, g, hq) in enumerate(heads):
            for qs in range(4):
                smm_at[(hidx, qs)] = acc
                acc += nchunk(t, qs)
        head_smm_end = {h: smm_at[(h, 3)] + nchunk(heads[h][0], 3) for h in range(64)}
        SMM_TOT = acc

        tr_at = {}
        tr = 0
        for t in range(4):
            for j in range(16):
                hidx = 16 * t + j
                for qs in range(4):
                    tr_at[("P", hidx, qs)] = tr
                    tr += 4 * nchunk(t, qs)
                tr_at[("Pend", hidx)] = tr
                if j >= 1:
                    tr_at[("Yend", hidx - 1)] = tr
                    tr += 4
            tr_at[("Yend", 16 * t + 15)] = tr
            tr += 4
            for qs in range(4):
                tr_at[("YR", t, qs)] = tr
                tr += 16
            for hc in range(16):
                tr_at[("Z", t, hc)] = tr
                tr += 4
        TR_TOT = tr
        trP_end = lambda h: tr_at[("Pend", h)]
        trY_end = lambda h: tr_at[("Yend", h)] + 4
        trYR_end = lambda t: tr_at[("YR", t, 3)] + 16
        trZ_end = lambda t, hc: tr_at[("Z", t, hc)] + 4

        aq = lambda hidx, qs: 4 * hidx + qs

        with nc.Block() as blk:
            @blk.sync
            def _(sync):
                for t in range(4):
                    kvt = (4 * t + 4) * 512
                    if t > 0 and 4 <= BCLVL < 11:
                        sync.wait_ge(b_gum, 128 * t)
                    for g in range(4):
                        gi = 4 * t + g
                        if gi > 0:
                            prev_last = 4 * gi - 1
                            sync.wait_ge(b_smm, head_smm_end[prev_last])
                            if 2 <= BCLVL < 11:
                                sync.wait_ge(b_pv, prev_last + 1)
                            else:
                                sync.wait_ge(b_ptev, trP_end(prev_last) // 2)
                        sync.dma_start(Ksb[:, 0:kvt], scr_K[g][:, 0:kvt]).then_inc(b_kv, 16)
                        sync.dma_start(shC[:, 8192:16384].rearrange("p (n d) -> p n d", d=128)[:, 0:kvt // 128, :],
                                       scr_V[0:kvt, 128 * g:128 * (g + 1)]
                                       .rearrange("(n p) d -> p n d", p=128)).then_inc(b_kv, 16)
                        for hq in range(4):
                            hidx = 4 * gi + hq
                            if hidx >= 2:
                                sync.wait_ge(b_smm, head_smm_end[hidx - 2])
                            if t > 0 and hidx % 16 < 2 and 5 <= BCLVL < 11:
                                sync.wait_ge(b_z, 16 * t)
                            src = scr_Q[4 * t:4 * t + 4, 4 * g + hq].rearrange("m d q -> d m q")
                            sync.dma_start(qsb[hidx % 2].rearrange("d (m q) -> d m q", q=128),
                                           src).then_inc(b_qld2[hidx % 2], 16)
                    if BCLVL < 4 or BCLVL >= 11:
                        continue
                    if t > 0:
                        sync.wait_ge(b_gum, 128 * t)
                    sync.dma_start(xnT_o[:].rearrange("h (c t2) -> h c t2", c=16),
                                   scr_XnT[:, :, 512 * t:512 * (t + 1)]
                                   .rearrange("c h t2 -> h c t2")).then_inc(b_xnt, 16)
                    for hc in range(16):
                        if BCLVL < 5 or BCLVL >= 11:
                            continue
                        pidx = 16 * t + hc
                        sync.wait_ge(b_gum, 128 * (t + 1))
                        if pidx >= 1:
                            sync.wait_ge(b_pmm, pidx)
                        sync.dma_start(wpof_b.rearrange("h (c o) -> h c o", c=16),
                                       scr_WpT[:, :, 128 * hc:128 * (hc + 1)]
                                       .rearrange("c h o -> h c o")).then_inc(b_wp, 16)
                        if pidx >= 1:
                            pt_, ph_ = divmod(pidx - 1, 16)
                            sync.wait_ge(b_ptev, trZ_end(pt_, ph_) // 2)
                        sync.dma_start(xres_b[:].rearrange("p (a h2) -> p a h2", a=4),
                                       x_own[512 * t:512 * (t + 1), 128 * hc:128 * (hc + 1)]
                                       .rearrange("(a p) h2 -> p a h2", p=128)).then_inc(b_xr, 16)
                        sync.wait_ge(b_ptev, trZ_end(t, hc) // 2)
                        sync.dma_start(out_own[512 * t:512 * (t + 1), 128 * hc:128 * (hc + 1)]
                                       .rearrange("(a p) h2 -> p a h2", p=128),
                                       ost3).then_inc(b_out, 16)
                if 5 <= BCLVL < 11:
                    sync.wait_ge(b_out, 16 * 64)
                elif 2 <= BCLVL < 11:
                    sync.wait_ge(b_pv, 64)
                else:
                    sync.wait_ge(b_ptev, TR_TOT // 2)

            @blk.gpsimd
            def _(g_):
                for t in range(4):
                    if BCLVL < 2 or BCLVL in (11, 12, 13, 14):
                        g_.nop().then_inc(b_pz, 1)
                        continue
                    if 5 <= BCLVL < 11:
                        g_.wait_ge(b_dmm, 16 * t)
                    g_.wait_ge(b_pv, 16 * t)
                    for qs in range(2):
                        nks = 12 - 4 * qs
                        ks0 = 4 * (4 * t + qs + 1)
                        g_.memset(PT3[:, ks0:ks0 + nks, 128 * qs:128 * (qs + 1)], 0.0)
                    g_.memset(PT3[:, 4 * (4 * t + 3):4 * (4 * t + 3) + 4, 256:384],
                              0.0).then_inc(b_pz, 1)
                    if BCLVL < 4 or BCLVL >= 11:
                        continue
                    g_.wait_ge(b_ptp, trYR_end(t))
                    g_.wait_ge(b_pv, 16 * (t + 1))
                    g_.wait_ge(b_smm, head_smm_end[16 * t + 15])
                    g_.wait_ge(b_pexp, head_smm_end[16 * t + 15])
                    if t > 0:
                        g_.wait_ge(b_pmm, 16 * t)
                    for oc in range(16):
                        if oc >= 2:
                            g_.wait_ge(b_gum, 128 * t + 8 * (oc - 1))
                        g_.dma_start(wgbuf[oc % 2].rearrange("p (c o) -> p c o", c=16),
                                     w_gate[:, 512 * oc:512 * (oc + 1)]
                                     .rearrange("(c p) o -> p c o", p=128)).then_inc(b_wg2[(16 * t + oc) % 2], 16)
                        g_.dma_start(wubuf[oc % 2].rearrange("p (c o) -> p c o", c=16),
                                     w_up[:, 512 * oc:512 * (oc + 1)]
                                     .rearrange("(c p) o -> p c o", p=128)).then_inc(b_wg2[(16 * t + oc) % 2], 16)
                    if BCLVL < 5 or BCLVL >= 11:
                        continue
                    g_.wait_ge(b_ptp, trP_end(16 * t + 15))
                    for hc in range(16):
                        pidx = 16 * t + hc
                        if pidx >= 2:
                            g_.wait_ge(b_dmm, pidx - 1)
                        g_.dma_start(wdbuf[hc % 2].rearrange("p (n h) -> p n h", n=64),
                                     w_down[:, 128 * hc:128 * (hc + 1)]
                                     .rearrange("(n p) h -> p n h", p=128)).then_inc(b_wd2[pidx % 2], 16)

            @blk.tensor
            def _(t_):
                def emit_S_pre(hidx):
                    t, g, hq = heads[hidx]
                    if hidx % 16 == 0 and 4 <= BCLVL < 11:
                        t_.wait_ge(b_sil, 64 * t)
                        t_.wait_ge(b_fu, 64 * t)
                    if hq == 0:
                        t_.wait_ge(b_kv, 16 * (2 * (4 * t + g) + 2))
                    t_.wait_ge(b_qld2[hidx % 2], 16 * (hidx // 2 + 1))

                def emit_S_chunks(hidx, qs, k0, k1):
                    t, g, hq = heads[hidx]
                    base = smm_at[(hidx, qs)]
                    for kc in range(k0, k1):
                        idx = base + kc
                        if idx >= 3:
                            t_.wait_ge(b_pexp, idx - 2)
                        t_.matmul(ps_s[idx % 3][:],
                                  qsb2[:, 512 * (hidx % 2) + 128 * qs:512 * (hidx % 2) + 128 * (qs + 1)],
                                  Ksb[:, 512 * kc:512 * (kc + 1)],
                                  start=True, stop=True).then_inc(b_smm, 1)

                def emit_tpP_qs(hidx, qs):
                    t, g, hq = heads[hidx]
                    n = nchunk(t, qs)
                    if BCLVL in (11, 12, 13, 14):
                        t_.nop().then_inc(b_ptp, 4 * n)
                        return
                    base = smm_at[(hidx, qs)]
                    tr0 = tr_at[("P", hidx, qs)]
                    for j in range(4 * n):
                        idx = tr0 + j
                        if j % 4 == 0:
                            t_.wait_ge(b_dg, base + j // 4 + 1)
                        if idx >= 4:
                            t_.wait_ge(b_ptev, (idx - 4) // 2 + 1)
                        dslot = 16 * (qs % 2) + j // 4
                        t_.matmul(tp_slot_f32(idx), Pb[:, 128 * j:128 * (j + 1)],
                                  diagb[:, 128 * dslot:128 * (dslot + 1)],
                                  start=True, stop=True).then_inc(b_ptp, 1)

                def emit_PV(hidx):
                    if BCLVL < 2 or BCLVL in (11, 12, 13, 14):
                        t_.nop().then_inc(b_pv, 1)
                        t_.nop().then_inc(b_ptp, 4)
                        return
                    t, g, hq = heads[hidx]
                    nks = (t + 1) * 16
                    p30 = tr_at[("P", hidx, 3)] // 2
                    t_.wait_ge(b_pz, t + 1)
                    if hidx % 16 == 0 and 5 <= BCLVL < 11:
                        t_.wait_ge(b_gsil, 16 * t)
                    if hidx > 0:
                        t_.wait_ge(b_yev, hidx)
                    for ks in range(nks - 1):
                        if ks % 2 == 0:
                            t_.wait_ge(b_ptev, p30 + ks // 2 + 1)
                        t_.matmul(ps_acc[:], Vtile(ks), PTtile(ks),
                                  start=(ks == 0), stop=False)
                    t_.matmul(ps_acc[:], Vtile(nks - 1), PTtile(nks - 1),
                              start=(nks == 1), stop=True).then_inc(b_pv, 1)

                def emit_ytr(hidx):
                    if BCLVL < 2 or BCLVL in (11, 12, 13, 14):
                        return
                    t_.wait_ge(b_yev, hidx + 1)
                    tr0 = tr_at[("Yend", hidx)]
                    for j in range(4):
                        idx = tr0 + j
                        if idx >= 4:
                            t_.wait_ge(b_ptev, (idx - 4) // 2 + 1)
                        t_.transpose(tp_slot(idx), ystage[:, 128 * j:128 * (j + 1)],
                                     idb[:]).then_inc(b_ptp, 1)

                for t in range(4):
                    for j in range(16):
                        h = 16 * t + j
                        if j >= 1:
                            emit_PV(h - 1)
                        emit_S_pre(h)
                        for qs in range(4):
                            n = nchunk(t, qs)
                            k1 = min(3, n)
                            emit_S_chunks(h, qs, 0, k1)
                            if qs >= 1:
                                emit_tpP_qs(h, qs - 1)
                            emit_S_chunks(h, qs, k1, n)
                        emit_tpP_qs(h, 3)
                        if j >= 1:
                            emit_ytr(h - 1)
                    emit_PV(16 * t + 15)
                    emit_ytr(16 * t + 15)
                    if BCLVL < 3 or BCLVL >= 11:
                        t_.nop().then_inc(b_ptp, 64)
                    for qs in range(4 if 3 <= BCLVL < 11 else 0):
                        t_.wait_ge(b_yrs, 4 * t + qs + 1)
                        tr0 = tr_at[("YR", t, qs)]
                        for j in range(16):
                            idx = tr0 + j
                            if idx >= 4:
                                t_.wait_ge(b_ptev, (idx - 4) // 2 + 1)
                            t_.transpose(tp_slot(idx),
                                         ybufq(qs)[:, 128 * j:128 * (j + 1)],
                                         idb[:]).then_inc(b_ptp, 1)
                    if BCLVL < 4 or BCLVL >= 11:
                        t_.nop().then_inc(b_ptp, 64)
                        t_.nop().then_inc(b_gum, 128)
                        continue
                    t_.wait_ge(b_xnt, 16 * (t + 1))
                    t_.wait_ge(b_pexp, head_smm_end[16 * t + 15])
                    for oc in range(16):
                        gidx = 16 * t + oc
                        t_.wait_ge(b_wg2[gidx % 2], 32 * (gidx // 2 + 1))
                        for osub in range(4):
                            o_lin = 64 * t + 4 * oc + osub
                            if o_lin > 0:
                                t_.wait_ge(b_sil, o_lin)
                            for c in range(15):
                                t_.matmul(ps_s[0][:],
                                          wgbuf[oc % 2][:, 512 * c + 128 * osub:512 * c + 128 * (osub + 1)],
                                          xn3(c), start=(c == 0), stop=False)
                            t_.matmul(ps_s[0][:],
                                      wgbuf[oc % 2][:, 512 * 15 + 128 * osub:512 * 15 + 128 * (osub + 1)],
                                      xn3(15), start=False, stop=True).then_inc(b_gum, 1)
                            if o_lin > 0:
                                t_.wait_ge(b_fu, o_lin)
                            for c in range(15):
                                t_.matmul(ps_s[1][:],
                                          wubuf[oc % 2][:, 512 * c + 128 * osub:512 * c + 128 * (osub + 1)],
                                          xn3(c), start=(c == 0), stop=False)
                            t_.matmul(ps_s[1][:],
                                      wubuf[oc % 2][:, 512 * 15 + 128 * osub:512 * 15 + 128 * (osub + 1)],
                                      xn3(15), start=False, stop=True).then_inc(b_gum, 1)
                    if BCLVL < 5 or BCLVL >= 11:
                        t_.nop().then_inc(b_ptp, 64)
                        t_.nop().then_inc(b_pmm, 16)
                        t_.nop().then_inc(b_dmm, 16)
                        continue
                    t_.wait_ge(b_ptev, trYR_end(t) // 2)
                    t_.wait_ge(b_yev, 16 * (t + 1))
                    for hc in range(16):
                        pidx = 16 * t + hc
                        t_.wait_ge(b_wp, 16 * (pidx + 1))
                        if pidx > 0:
                            t_.wait_ge(b_gsil, pidx)
                        for c in range(15):
                            t_.matmul(ps_acc[:], wpof_b[:, 128 * c:128 * (c + 1)],
                                      yrTt(c), start=(c == 0), stop=False)
                        t_.matmul(ps_acc[:], wpof_b[:, 1920:2048], yrTt(15),
                                  start=False, stop=True).then_inc(b_pmm, 1)
                        t_.wait_ge(b_sev, pidx + 1)
                        t_.wait_ge(b_fu, 64 * (t + 1))
                        t_.wait_ge(b_wd2[pidx % 2], 16 * (pidx // 2 + 1))
                        for osub in range(63):
                            t_.matmul(ps_acc[:], wdbuf[hc % 2][:, 128 * osub:128 * (osub + 1)],
                                      fuseTt(osub), start=(osub == 0), stop=False)
                        t_.matmul(ps_acc[:], wdbuf[hc % 2][:, 8064:8192], fuseTt(63),
                                  start=False, stop=True).then_inc(b_dmm, 1)
                        t_.wait_ge(b_z, pidx + 1)
                        tr0 = tr_at[("Z", t, hc)]
                        for j in range(4):
                            idx = tr0 + j
                            if idx >= 4:
                                t_.wait_ge(b_ptev, (idx - 4) // 2 + 1)
                            t_.transpose(tp_slot_f32(idx), gT[:, 128 * j:128 * (j + 1)],
                                         idf[:]).then_inc(b_ptp, 1)

            @blk.vector
            def _(v):
                def emit_ydve(hy):
                    v.wait_ge(b_pv, hy + 1)
                    if hy > 0:
                        v.wait_ge(b_ptp, trY_end(hy - 1))
                    v.tensor_copy(ystage[:], ps_acc[:]).then_inc(b_yev, 1)
                    tr0 = tr_at[("Yend", hy)]
                    hh = 4 * heads[hy][1] + heads[hy][2]
                    hb = 8 + 4 * (hy % 2)
                    for j in range(4):
                        idx = tr0 + j
                        v.wait_ge(b_ptp, tr0 + 2 * (j // 2) + 2)
                        op = v.tensor_scalar(ybufq(j)[:, 128 * hh:128 * (hh + 1)],
                                             tp_slot(idx), sct[:, hb + j:hb + j + 1],
                                             None, ALU.mult)
                        if j % 2 == 1:
                            op.then_inc(b_ptev, 1)

                for t in range(4):
                    for j in range(16):
                        hidx = 16 * t + j
                        pend = []
                        for qs in range(4):
                            n = nchunk(t, qs)
                            m = 4 * t + qs
                            base = smm_at[(hidx, qs)]
                            par = 16 * (qs % 2)
                            a = aq(hidx, qs)
                            if hidx % 16 == 0 and qs == 0 and 4 <= BCLVL < 11:
                                v.wait_ge(b_gum, 128 * t)
                                if BCLVL >= 5:
                                    v.wait_ge(b_dmm, 16 * t)
                            if a >= 2:
                                v.wait_ge(b_pn, a - 1)
                            for kc in range(n):
                                idx = base + kc
                                v.wait_ge(b_smm, idx + 1)
                                if kc == m:
                                    v.scalar_tensor_tensor(ps_s[idx % 3][:], ps_s[idx % 3][:],
                                                           1.0, dmask_t[:], ALU.mult, ALU.add)
                                    v.drain()
                                v.reduce_max(ncm[:, par + kc:par + kc + 1], ps_s[idx % 3][:],
                                             axis=AXX, negate=True).then_inc(b_cm, 1)
                                if kc < 2:
                                    for _ in range(2):
                                        if pend:
                                            pend.pop(0)()
                                else:
                                    while pend:
                                        pend.pop(0)()
                            while pend:
                                pend.pop(0)()
                            v.wait_ge(b_pexp, base + n)
                            v.drain()
                            v.tensor_reduce(nm_fin[:], ncm[:, par:par + n], axis=AXX, op=ALU.min)
                            v.drain()
                            v.tensor_scalar(dlt[:, par:par + n], ncm[:, par:par + n],
                                            nm_fin[:], None, ALU.subtract).then_inc(b_sm1, 1)
                            v.wait_ge(b_ed, a + 1)
                            dg0 = 16 * (qs % 2)
                            for kc in range(n):
                                v.tensor_scalar(diagb[:, 128 * (dg0 + kc):128 * (dg0 + kc + 1)],
                                                idb[:],
                                                edt[:, par + kc:par + kc + 1], None,
                                                ALU.mult).then_inc(b_dg, 1)
                            v.scalar_tensor_tensor(sct[:, 16:16 + n], dens[:, par:par + n],
                                                   1.0, edt[:, par:par + n], ALU.mult, ALU.mult,
                                                   accum_out=sct[:, 4 * (hidx % 2) + qs:
                                                                4 * (hidx % 2) + qs + 1])
                            v.drain()
                            v.nop().then_inc(b_pn, 1)
                            if qs == 0:
                                v.wait_ge(b_pv, hidx)
                                if hidx % 16 == 0:
                                    v.wait_ge(b_pz, t + 1)
                                    if 5 <= BCLVL < 11:
                                        v.wait_ge(b_dmm, 16 * t)
                            tr0 = tr_at[("P", hidx, qs)]

                            def mk_evac(idx_, j2_, qs_):
                                def f():
                                    v.wait_ge(b_ptp, idx_ + 2)
                                    v.tensor_copy(PT3[:, 2 * j2_:2 * j2_ + 2,
                                                      128 * qs_:128 * (qs_ + 1)],
                                                  tp_pair_f32(idx_)).then_inc(b_ptev, 1)
                                return f

                            evs = [mk_evac(tr0 + 2 * j2, j2, qs) for j2 in range(2 * n)]
                            if qs == 3:
                                for f in evs:
                                    f()
                                hb0 = 4 * (hidx % 2)
                                v.drain()
                                for q_ in range(4):
                                    v.reciprocal(sct[:, 8 + hb0 + q_:8 + hb0 + q_ + 1],
                                                 sct[:, hb0 + q_:hb0 + q_ + 1])
                                    v.drain()
                            else:
                                pend = evs
                        if BCLVL < 2 or BCLVL in (11, 12, 13, 14):
                            v.nop().then_inc(b_yev, 1)
                            v.nop().then_inc(b_ptev, 2)
                            continue
                        if j >= 1:
                            emit_ydve(hidx - 1)
                    if not (BCLVL < 2 or BCLVL in (11, 12, 13, 14)):
                        emit_ydve(16 * t + 15)
                    if BCLVL < 3 or BCLVL >= 11:
                        v.nop().then_inc(b_rsq, 4)
                        v.nop().then_inc(b_yrs, 4)
                        v.nop().then_inc(b_ptev, 32)
                        v.nop().then_inc(b_fu, 64)
                        v.nop().then_inc(b_sev, 16)
                        v.nop().then_inc(b_z, 16)
                        v.nop().then_inc(b_ptev, 32)
                        continue
                    v.wait_ge(b_ptp, trY_end(16 * t + 15))
                    v.drain()
                    for qs in range(4):
                        yq = ybufq(qs)
                        for q4 in range(4):
                            if q4 > 0:
                                v.drain()
                            v.scalar_tensor_tensor(rms_scr[:], yq[:, 512 * q4:512 * (q4 + 1)],
                                                   1.0, yq[:, 512 * q4:512 * (q4 + 1)],
                                                   ALU.mult, ALU.mult,
                                                   accum_out=ssq_p[:, q4:q4 + 1])
                        v.drain()
                        v.tensor_reduce(ssq[:], ssq_p[:], axis=AXX, op=ALU.add).then_inc(b_rsq, 1)
                        v.wait_ge(b_yrstd, 4 * t + qs + 1)
                        v.drain()
                        v.tensor_scalar(yq[:], yq[:], yrstd[:], None, ALU.mult).then_inc(b_yrs, 1)
                        if qs == 0 and t > 0:
                            v.wait_ge(b_pmm, 16 * t)
                        tr0 = tr_at[("YR", t, qs)]
                        for j2 in range(8):
                            idx = tr0 + 2 * j2
                            v.wait_ge(b_ptp, idx + 2)
                            v.tensor_copy(yr3[:, 2 * j2:2 * j2 + 2, 128 * qs:128 * (qs + 1)],
                                          tp_pair(idx)).then_inc(b_ptev, 1)
                    if BCLVL < 4 or BCLVL >= 11:
                        v.nop().then_inc(b_fu, 64)
                        v.nop().then_inc(b_sev, 16)
                        v.nop().then_inc(b_z, 16)
                        v.nop().then_inc(b_ptev, 32)
                        continue
                    v.wait_ge(b_pv, 16 * (t + 1))
                    for oc in range(16):
                        for osub in range(4):
                            o = 4 * oc + osub
                            o_lin = 64 * t + o
                            v.wait_ge(b_sil, o_lin + 1)
                            v.wait_ge(b_gum, 128 * t + 8 * oc + 2 * osub + 2)
                            v.scalar_tensor_tensor(fuseTt(o), ps_s[1][:],
                                                   su_col[:, o:o + 1], sil[:],
                                                   ALU.mult, ALU.mult).then_inc(b_fu, 1)
                    if BCLVL < 5 or BCLVL >= 11:
                        if 4 <= BCLVL < 11:
                            v.nop().then_inc(b_sev, 16)
                            v.nop().then_inc(b_z, 16)
                            v.nop().then_inc(b_ptev, 32)
                        continue
                    for hc in range(16):
                        pidx = 16 * t + hc
                        v.wait_ge(b_pmm, pidx + 1)
                        v.tensor_copy(sT_hc, ps_acc[:]).then_inc(b_sev, 1)
                        v.wait_ge(b_gsil, pidx + 1)
                        v.drain()
                        v.tensor_tensor(gT[:], gT[:], sT_hc, ALU.mult).then_inc(b_z, 1)
                        v.wait_ge(b_xr, 16 * (pidx + 1))
                        if pidx > 0:
                            v.wait_ge(b_out, 16 * pidx)
                        tr0 = tr_at[("Z", t, hc)]
                        for j2 in range(2):
                            idx = tr0 + 2 * j2
                            v.wait_ge(b_ptp, idx + 2)
                            v.tensor_tensor(ost3[:, 2 * j2:2 * j2 + 2, :],
                                            tp_pair_f32(idx),
                                            xres_b[:].rearrange("p (a h2) -> p a h2", a=4)[:, 2 * j2:2 * j2 + 2, :],
                                            ALU.add).then_inc(b_ptev, 1)

            @blk.scalar
            def _(s):
                for t in range(4):
                    for j in range(16):
                        hidx = 16 * t + j
                        for qs in range(4):
                            n = nchunk(t, qs)
                            base = smm_at[(hidx, qs)]
                            par = 16 * (qs % 2)
                            a = aq(hidx, qs)
                            if hidx % 16 == 0 and qs == 0 and 4 <= BCLVL < 11:
                                s.wait_ge(b_gum, 128 * t)
                                if 5 <= BCLVL < 11:
                                    s.wait_ge(b_dmm, 16 * t)
                            if qs > 0:
                                tr0p, npv = tr_at[("P", hidx, qs - 1)], nchunk(t, qs - 1)
                            elif hidx > 0:
                                tr0p, npv = tr_at[("P", hidx - 1, 3)], nchunk(heads[hidx - 1][0], 3)
                            else:
                                tr0p, npv = None, 0
                            if a >= 2:
                                s.wait_ge(b_pn, a - 1)
                            for kc in range(n):
                                idx = base + kc
                                if tr0p is not None:
                                    s.wait_ge(b_ptp, tr0p + 4 * min(kc + 1, npv))
                                s.wait_ge(b_cm, idx + 1)
                                s.activation(Pb[:, 512 * kc:512 * (kc + 1)], ps_s[idx % 3][:],
                                             AF.Exp, bias=ncm[:, par + kc:par + kc + 1],
                                             scale=1.0,
                                             accum_out=dens[:, par + kc:par + kc + 1]
                                             ).then_inc(b_pexp, 1)
                            s.wait_ge(b_sm1, a + 1)
                            s.drain()
                            s.activation(edt[:, par:par + n], dlt[:, par:par + n], AF.Exp,
                                         scale=-1.0).then_inc(b_ed, 1)
                            s.drain()
                    if BCLVL < 3 or BCLVL >= 11:
                        s.nop().then_inc(b_yrstd, 4)
                    for qs in range(4 if 3 <= BCLVL < 11 else 0):
                        s.wait_ge(b_rsq, 4 * t + qs + 1)
                        s.activation(yrstd[:], ssq[:], AF.Ln, bias=eps_rms_t[:],
                                     scale=1.0 / 2048.0)
                        s.drain()
                        s.activation(yrstd[:], yrstd[:], AF.Exp, scale=-0.5)
                        s.drain()
                        s.nop().then_inc(b_yrstd, 1)
                    if BCLVL < 4 or BCLVL >= 11:
                        s.nop().then_inc(b_sil, 64)
                        s.nop().then_inc(b_gsil, 16)
                    for oc in range(16 if 4 <= BCLVL < 11 else 0):
                        for osub in range(4):
                            o = 4 * oc + osub
                            o_lin = 64 * t + o
                            s.wait_ge(b_gum, 128 * t + 8 * oc + 2 * osub + 1)
                            if o_lin > 0:
                                s.wait_ge(b_fu, o_lin)
                            if oc == 0 and osub == 0 and t > 0:
                                s.wait_ge(b_z, 16 * t)
                                s.wait_ge(b_ptp, trZ_end(t - 1, 15))
                            s.activation(sil[:], ps_s[0][:], AF.Silu,
                                         scale=sg_col[:, o:o + 1]).then_inc(b_sil, 1)
                    if 4 <= BCLVL < 5:
                        s.nop().then_inc(b_gsil, 16)
                    for hc in range(16 if 5 <= BCLVL < 11 else 0):
                        pidx = 16 * t + hc
                        s.wait_ge(b_dmm, pidx + 1)
                        if pidx > 0:
                            s.wait_ge(b_z, pidx)
                            s.wait_ge(b_ptp, trZ_end(*divmod(pidx - 1, 16)))
                        s.activation(gT[:], ps_acc[:], AF.Silu,
                                     scale=sd_col[:, hc:hc + 1]).then_inc(b_gsil, 1)



def _host_tables():
    inv_freq = (1.0 / 10000.0 ** (np.arange(0, HD, 2, dtype=np.float32) / HD)).astype(np.float32)
    t = np.arange(S, dtype=np.float32)
    freqs = t[:, None] * inv_freq[None, :]
    cos = np.cos(freqs.astype(np.float64)).astype(np.float32)
    sin = np.sin(freqs.astype(np.float64)).astype(np.float32)
    return cos, sin


def _own_rows(r):
    idx = []
    for m in range(NM):
        g0 = (r + 4 * m) * 128
        idx.extend(range(g0, g0 + 128))
    return np.asarray(idx)


def _diag_mask(r):
    j = np.arange(512)[None, :]
    i = np.arange(128)[:, None]
    return np.where(j <= r * 128 + i, 0.0, NEG_BIG).astype(np.float32)


def make_in_maps(x, w_qkv, w_proj, q_gain, w_gate, w_up, w_down, s_gate, s_up, s_down):
    cos, sin = _host_tables()
    in_maps = []
    for c in range(8):
        b, r = c // 4, c % 4
        rows = _own_rows(r)
        in_maps.append(dict(
            xb=np.ascontiguousarray(x[b]),
            x_own=np.ascontiguousarray(x[b][rows]),
            w_qkv=np.ascontiguousarray(np.asarray(w_qkv, np.float32)),
            w_proj=np.ascontiguousarray(np.asarray(w_proj, np.float32)),
            q_gain=np.ascontiguousarray(np.asarray(q_gain, np.float32)),
            w_gate=np.ascontiguousarray(np.asarray(w_gate, np.int8)),
            w_up=np.ascontiguousarray(np.asarray(w_up, np.int8)),
            w_down=np.ascontiguousarray(np.asarray(w_down, np.int8)),
            s_gate=np.ascontiguousarray(np.asarray(s_gate, np.float32)),
            s_up=np.ascontiguousarray(np.asarray(s_up, np.float32)),
            s_down=np.ascontiguousarray(np.asarray(s_down, np.float32)),
            rope_c=cos, rope_s=sin,
            rope_co=np.ascontiguousarray(cos[rows]),
            rope_so=np.ascontiguousarray(sin[rows]),
            dmask=_diag_mask(r).astype(ml_dtypes.bfloat16),
        ))
    return in_maps


def kernel(L0_attn_x, L0_attn_w_qkv, L0_attn_w_proj, L0_attn_q_gain,
           L0_ffn_w_gate, L0_ffn_w_up, L0_ffn_w_down,
           L0_ffn_s_gate, L0_ffn_s_up, L0_ffn_s_down):
    x = np.ascontiguousarray(np.asarray(L0_attn_x, dtype=np.float32))
    nc = build_program("full")
    in_maps = make_in_maps(x, L0_attn_w_qkv, L0_attn_w_proj, L0_attn_q_gain,
                           L0_ffn_w_gate, L0_ffn_w_up, L0_ffn_w_down,
                           L0_ffn_s_gate, L0_ffn_s_up, L0_ffn_s_down)
    res = run_bass_kernel_spmd(nc, in_maps, core_ids=list(range(8)))
    out = np.empty((B, S, D), dtype=np.float32)
    for c in range(8):
        b, r = c // 4, c % 4
        out[b][_own_rows(r)] = res.results[c]["out_own"]
    return out


# version 12 (final)
# speedup vs baseline: 1.1996x; 1.1996x over previous
import numpy as np
import ml_dtypes
from contextlib import ExitStack

import concourse.bass as bass
import concourse.mybir as mybir
from concourse.bass_utils import run_bass_kernel_spmd

F32 = mybir.dt.float32
BF16 = mybir.dt.bfloat16
I8 = mybir.dt.int8
AF = mybir.ActivationFunctionType
ALU = mybir.AluOpType
AXX = mybir.AxisListType.X

B, S, D = 2, 8192, 2048
NH, NKV, HD = 16, 4, 128
INTER = 8192
NM = 16
NTOK_OWN = NM * 128
SCALE = float(1.0 / np.sqrt(128.0))
EPS_LN = 1e-5
EPS_RMS = 1.1920929e-07
BF16_1EM8 = float(np.float32(ml_dtypes.bfloat16(1e-8)))
NEG_BIG = -1e30
A1LVL = 99
BCLVL = 5


def build_program(upto="full", nit=64, nm2=16):
    nc = bass.Bass("TRN2", target_bir_lowering=False)

    xb = nc.dram_tensor("xb", [S, D], F32, kind="ExternalInput").ap()
    x_own = nc.dram_tensor("x_own", [NTOK_OWN, D], F32, kind="ExternalInput").ap()
    w_qkv = nc.dram_tensor("w_qkv", [3072, D], F32, kind="ExternalInput").ap()
    w_proj = nc.dram_tensor("w_proj", [D, D], F32, kind="ExternalInput").ap()
    q_gain = nc.dram_tensor("q_gain", [NH], F32, kind="ExternalInput").ap()
    w_gate = nc.dram_tensor("w_gate", [D, INTER], I8, kind="ExternalInput").ap()
    w_up = nc.dram_tensor("w_up", [D, INTER], I8, kind="ExternalInput").ap()
    w_down = nc.dram_tensor("w_down", [INTER, D], I8, kind="ExternalInput").ap()
    s_gate = nc.dram_tensor("s_gate", [INTER], F32, kind="ExternalInput").ap()
    s_up = nc.dram_tensor("s_up", [INTER], F32, kind="ExternalInput").ap()
    s_down = nc.dram_tensor("s_down", [D], F32, kind="ExternalInput").ap()
    rope_c = nc.dram_tensor("rope_c", [S, 64], F32, kind="ExternalInput").ap()
    rope_s = nc.dram_tensor("rope_s", [S, 64], F32, kind="ExternalInput").ap()
    rope_co = nc.dram_tensor("rope_co", [NTOK_OWN, 64], F32, kind="ExternalInput").ap()
    rope_so = nc.dram_tensor("rope_so", [NTOK_OWN, 64], F32, kind="ExternalInput").ap()
    dmask = nc.dram_tensor("dmask", [128, 512], BF16, kind="ExternalInput").ap()

    out_own = nc.dram_tensor("out_own", [NTOK_OWN, D], F32, kind="ExternalOutput").ap()

    sk = "ExternalOutput" if upto in ("A", "A1") else "Internal"
    scr_K = nc.dram_tensor("scr_K", [NKV, 128, S], BF16, kind=sk).ap()
    scr_V = nc.dram_tensor("scr_V", [S, 512], BF16, kind=sk).ap()
    scr_Q = nc.dram_tensor("scr_Q", [NM, NH, 128, 128], BF16, kind=sk).ap()
    scr_XnT = nc.dram_tensor("scr_XnT", [16, 128, NTOK_OWN], BF16, kind=sk).ap()
    scr_WpT = nc.dram_tensor("scr_WpT", [16, 128, D], BF16, kind=sk).ap()

    root = ExitStack()
    with root:
        sb = lambda name, shape, dt: root.enter_context(nc.sbuf_tensor(name, shape, dt))
        sem = lambda name: root.enter_context(nc.semaphore(name))

        idb = sb("idb", [128, 128], BF16)
        idf = sb("idf", [128, 128], F32)
        eps_ln_t = sb("eps_ln_t", [128, 1], F32)
        eps_rms_t = sb("eps_rms_t", [128, 1], F32)
        gainb = sb("gainb", [128, NH], F32)
        sg_col = sb("sg_col", [128, 64], F32)
        su_col = sb("su_col", [128, 64], F32)
        sd_col = sb("sd_col", [128, 16], F32)
        dmask_t = sb("dmask_t", [128, 512], BF16)

        s_c0 = sem("s_c0")
        s_c1 = sem("s_c1")

        rows_scope = ExitStack()
        sg_row = rows_scope.enter_context(nc.sbuf_tensor("sg_row", [64, 128], F32))
        su_row = rows_scope.enter_context(nc.sbuf_tensor("su_row", [64, 128], F32))
        sd_row = rows_scope.enter_context(nc.sbuf_tensor("sd_row", [16, 128], F32))

        with nc.Block() as blk:
            @blk.sync
            def _(sync):
                sync.dma_start(gainb[:], q_gain.unsqueeze(0).broadcast_to([128, NH])).then_inc(s_c0, 16)
                sync.dma_start(dmask_t[:], dmask).then_inc(s_c0, 16)
                sync.dma_start(sg_row[:], s_gate.rearrange("(a p) -> a p", p=128)).then_inc(s_c0, 16)
                sync.dma_start(su_row[:], s_up.rearrange("(a p) -> a p", p=128)).then_inc(s_c0, 16)
                sync.dma_start(sd_row[:], s_down.rearrange("(a p) -> a p", p=128)).then_inc(s_c0, 16)

            @blk.gpsimd
            def _(g):
                g.memset(idb[:], 0.0)
                g.memset(idf[:], 0.0)
                g.drain()
                g.affine_select(idb[:], idb[:], [[-1, 128]], ALU.not_equal, 1.0,
                                base=0, channel_multiplier=1)
                g.affine_select(idf[:], idf[:], [[-1, 128]], ALU.not_equal, 1.0,
                                base=0, channel_multiplier=1)
                g.memset(eps_ln_t[:], EPS_LN)
                g.memset(eps_rms_t[:], EPS_RMS)

        with ExitStack() as csc:
            pb0 = csc.enter_context(nc.psum_tensor("pb_c0", [128, 512], F32))
            pb1 = csc.enter_context(nc.psum_tensor("pb_c1", [128, 512], F32))
            with nc.Block() as blk:
                @blk.tensor
                def _(t):
                    t.wait_ge(s_c0, 16 * 5)
                    t.transpose(pb0[:, 0:64], sg_row[:], idf[0:64, 0:64]).then_inc(s_c1, 1)
                    t.transpose(pb1[:, 0:64], su_row[:], idf[0:64, 0:64]).then_inc(s_c1, 1)

                @blk.vector
                def _(v):
                    v.wait_ge(s_c0, 16 * 5)
                    v.tensor_scalar(gainb[:], gainb[:], SCALE, None, ALU.mult)
                    v.wait_ge(s_c1, 2)
                    v.tensor_copy(sg_col[:], pb0[:, 0:64])
                    v.tensor_copy(su_col[:], pb1[:, 0:64]).then_inc(s_c1, 1)

            with nc.Block() as blk:
                @blk.tensor
                def _(t):
                    t.transpose(pb0[:, 0:16], sd_row[:], idf[0:16, 0:16]).then_inc(s_c1, 1)

                @blk.vector
                def _(v):
                    v.wait_ge(s_c1, 4)
                    v.tensor_copy(sd_col[:], pb0[:, 0:16])
        rows_scope.close()

        wqkv_scope = ExitStack()
        wqkvT = [wqkv_scope.enter_context(nc.sbuf_tensor(f"wqkvT{g}", [128, 3072], BF16))
                 for g in range(16)]

        s_p0sp = sem("s_p0sp")
        with ExitStack() as p0:
            p0sb = lambda name, shape, dt: p0.enter_context(nc.sbuf_tensor(name, shape, dt))
            wbb = [p0sb(f"p0_wb{i}", [128, 2048], BF16) for i in range(2)]
            gsum = p0sb("p0_gsum", [128, 16], F32)
            scale_bf = p0sb("p0_scalebf", [128, 16], BF16)
            scale_f = p0sb("p0_scalef", [128, 16], F32)
            r0 = p0sb("p0_r0", [128, 16], F32)
            rr = p0sb("p0_r", [128, 16], F32)
            tmp16 = p0sb("p0_tmp16", [128, 16], F32)
            tqt = p0sb("p0_tq", [128, 2048], BF16)
            posb = p0sb("p0_pos", [128, 2048], BF16)
            negb = p0sb("p0_neg", [128, 2048], BF16)
            qd = p0sb("p0_qd", [128, 2048], BF16)
            wtt = p0sb("p0_wt", [128, 2048], BF16)
            wp_stage = p0sb("p0_wpst", [128, 2048], BF16)
            tp0 = [p0.enter_context(nc.psum_tensor(f"p0_tp{i}", [128, 1024], BF16))
                   for i in range(4)]

            s_wld = [sem("s_wld0"), sem("s_wld1")]
            s_tern = sem("s_tern")
            s_p0tp = sem("s_p0tp")
            s_p0ev = sem("s_p0ev")

            NQT, NPT = 24, 16
            NT0 = NQT + NPT

            def wtile_src(i):
                if i < NQT:
                    return w_qkv[128 * i:128 * (i + 1), :]
                j = i - NQT
                return w_proj[128 * j:128 * (j + 1), :]

            with nc.Block() as blk:
                @blk.gpsimd
                def _(g):
                    for i in range(NT0):
                        if i >= 2:
                            g.wait_ge(s_tern, i - 1)
                        g.dma_start(wbb[i % 2][:], wtile_src(i)).then_inc(s_wld[i % 2], 16)

                @blk.vector
                def _(v):
                    for i in range(NT0):
                        v.wait_ge(s_wld[i % 2], 16 * (i // 2 + 1))
                        w = wbb[i % 2]
                        v.tensor_reduce(gsum[:], w[:].rearrange("p (a b) -> p a b", b=128),
                                        axis=AXX, op=ALU.add, apply_absolute_value=True)
                        v.drain()
                        v.tensor_scalar(scale_bf[:], gsum[:], 1.0 / 128.0, None, ALU.mult)
                        v.drain()
                        v.tensor_scalar(scale_bf[:], scale_bf[:], BF16_1EM8, None, ALU.max)
                        v.drain()
                        v.tensor_copy(scale_f[:], scale_bf[:])
                        v.drain()
                        v.reciprocal(r0[:], scale_f[:])
                        v.drain()
                        v.tensor_tensor(tmp16[:], scale_f[:], r0[:], ALU.mult)
                        v.drain()
                        v.tensor_scalar(tmp16[:], tmp16[:], -1.0, 2.0, ALU.mult, ALU.add)
                        v.drain()
                        v.tensor_tensor(rr[:], r0[:], tmp16[:], ALU.mult)
                        v.drain()
                        for g in range(16):
                            v.tensor_scalar(tqt[:, 128 * g:128 * (g + 1)],
                                            w[:, 128 * g:128 * (g + 1)],
                                            rr[:, g:g + 1], None, ALU.mult)
                        v.drain()
                        v.tensor_scalar(posb[:], tqt[:], 0.5, None, ALU.is_gt)
                        v.tensor_scalar(negb[:], tqt[:], -0.5, None, ALU.is_lt)
                        v.drain()
                        v.tensor_tensor(qd[:], posb[:], negb[:], ALU.subtract)
                        v.drain()
                        for g in range(15):
                            v.tensor_scalar(wtt[:, 128 * g:128 * (g + 1)],
                                            qd[:, 128 * g:128 * (g + 1)],
                                            scale_f[:, g:g + 1], None, ALU.mult)
                        v.tensor_scalar(wtt[:, 1920:2048], qd[:, 1920:2048],
                                        scale_f[:, 15:16], None, ALU.mult).then_inc(s_tern, 1)
                        if i > NQT:
                            v.wait_ge(s_p0sp, 16 * (i - NQT))
                        for g in range(16):
                            n = 16 * i + g + 1
                            v.wait_ge(s_p0tp, n)
                            src = tp0[(n - 1) % 4][:, 0:128]
                            if i < NQT:
                                v.tensor_copy(wqkvT[g][:, 128 * i:128 * (i + 1)], src).then_inc(s_p0ev, 1)
                            else:
                                v.tensor_copy(wp_stage[:, 128 * g:128 * (g + 1)], src).then_inc(s_p0ev, 1)

                @blk.tensor
                def _(t):
                    for i in range(NT0):
                        t.wait_ge(s_tern, i + 1)
                        for g in range(16):
                            n = 16 * i + g + 1
                            if n > 4:
                                t.wait_ge(s_p0ev, n - 4)
                            t.transpose(tp0[(n - 1) % 4][:, 0:128],
                                        wtt[:, 128 * g:128 * (g + 1)], idb[:]).then_inc(s_p0tp, 1)

                @blk.sync
                def _(sync):
                    for j in range(NPT):
                        i = NQT + j
                        sync.wait_ge(s_p0ev, 16 * (i + 1))
                        dst = scr_WpT[:, :, 128 * j:128 * (j + 1)].rearrange("c h t -> h c t")
                        sync.dma_start(dst, wp_stage[:].rearrange("h (c t) -> h c t", t=128)
                                       ).then_inc(s_p0sp, 16)

        if upto == "0":
            dbg = nc.dram_tensor("dbg_wqkvT", [16, 128, 3072], BF16, kind="ExternalOutput").ap()
            s_dbg = sem("s_dbg")
            with nc.Block() as blk:
                @blk.sync
                def _(sync):
                    for g in range(16):
                        sync.dma_start(dbg[g], wqkvT[g][:]).then_inc(s_dbg, 16)
                    sync.wait_ge(s_dbg, 256)
                    sync.wait_ge(s_p0sp, 16 * 16)
            wqkv_scope.close()
            return nc

        with nc.Block() as blk:
            @blk.sync
            def _(sync):
                sync.wait_ge(s_p0sp, 16 * 16)

        with ExitStack() as pa:
            asb = lambda name, shape, dt: pa.enter_context(nc.sbuf_tensor(name, shape, dt))
            xbuf = [asb(f"a_x{i}", [128, 2048], F32) for i in range(2)]
            xsum = asb("a_xsum", [128, 1], F32)
            xsq = asb("a_xsq", [128, 1], F32)
            meant = asb("a_mean", [128, 1], F32)
            m2t = asb("a_m2", [128, 1], F32)
            vart = asb("a_var", [128, 1], F32)
            rstd = asb("a_rstd", [128, 1], F32)
            nbias = asb("a_nbias", [128, 1], F32)
            xn = asb("a_xn", [128, 2048], BF16)
            xnT = asb("a_xnT", [128, 2048], BF16)
            ksq = asb("a_ksq", [128, 4], F32)
            sq_scr = asb("a_sqscr", [128, 512], F32)
            krstd = asb("a_krstd", [128, 4], F32)
            kn = asb("a_kn", [128, 512], F32)
            rt1 = asb("a_rt1", [128, 2048], F32)
            rt2 = asb("a_rt2", [128, 2048], F32)
            kro = asb("a_kro", [128, 512], BF16)
            ktst = asb("a_ktst", [128, 2048], BF16)
            vstage = [asb(f"a_vst{i}", [128, 512], BF16) for i in range(2)]
            costab = asb("a_cos", [128, 4096], F32)
            sintab = asb("a_sin", [128, 4096], F32)
            qcostab = asb("a_qcos", [128, 1024], F32)
            qsintab = asb("a_qsin", [128, 1024], F32)
            qsq = asb("a_qsq", [128, 16], F32)
            qrstd = asb("a_qrstd", [128, 16], F32)
            qn = asb("a_qn", [128, 2048], F32)
            qro = asb("a_qro", [128, 2048], BF16)
            qtst = asb("a_qtst", [128, 2048], BF16)

            ps_k = pa.enter_context(nc.psum_tensor("a_psk", [128, 512], F32))
            ps_v = pa.enter_context(nc.psum_tensor("a_psv", [128, 512], F32))
            tp = [pa.enter_context(nc.psum_tensor(f"a_tp{i}", [128, 1024], BF16))
                  for i in range(4)]
            ps_q = [ps_k, ps_v] + [pa.enter_context(nc.psum_tensor(f"a_psq{i}", [128, 512], F32))
                                   for i in range(2)]

            s_ax = [sem("s_ax0"), sem("s_ax1")]
            s_tab = sem("s_tab")
            s_var = sem("s_var")
            s_rstd = sem("s_rstd")
            s_nb = sem("s_nb")
            s_xn = sem("s_xn")
            s_tp = sem("s_tp")
            s_ev = sem("s_ev")
            s_mmk = sem("s_mmk")
            s_krstd = sem("s_krstd")
            s_kn = sem("s_kn")
            s_kro = sem("s_kro")
            s_vc = sem("s_vc")
            s_ksp = sem("s_ksp")
            s_vsp = [sem("s_vsp0"), sem("s_vsp1")]

            NIT = nit
            TABS = 64

            with nc.Block() as blk:
                @blk.sync
                def _(sync):
                    sync.dma_start(costab[:], rope_c.rearrange("(a p) d -> p a d", p=128)).then_inc(s_tab, 16)
                    sync.dma_start(sintab[:], rope_s.rearrange("(a p) d -> p a d", p=128)).then_inc(s_tab, 16)
                    sync.dma_start(qcostab[:], rope_co.rearrange("(a p) d -> p a d", p=128)).then_inc(s_tab, 16)
                    sync.dma_start(qsintab[:], rope_so.rearrange("(a p) d -> p a d", p=128)).then_inc(s_tab, 16)
                    for i in range(NIT):
                        if i >= 2:
                            sync.wait_ge(s_xn if A1LVL >= 2 else s_rstd, i - 1)
                        sync.dma_start(xbuf[i % 2][:], xb[128 * i:128 * (i + 1), :]).then_inc(s_ax[i % 2], 16)

                @blk.gpsimd
                def _(g):
                    for i in range(NIT):
                        if A1LVL >= 3:
                            g.wait_ge(s_vc, i + 1)
                            g.dma_start(scr_V[128 * i:128 * (i + 1), :], vstage[i % 2][:]).then_inc(s_vsp[i % 2], 16)
                        if A1LVL >= 4 and i % 4 == 3:
                            g.wait_ge(s_ev, 20 * (i + 1))
                            for hd in range(4):
                                g.dma_start(scr_K[hd][:, 128 * (i - 3):128 * (i + 1)],
                                            ktst[:, 512 * hd:512 * (hd + 1)]).then_inc(s_ksp, 16)

                @blk.vector
                def _(v):
                    for i in range(NIT):
                        if i == 0:
                            v.wait_ge(s_tab, 64)
                        v.wait_ge(s_ax[i % 2], 16 * (i // 2 + 1))
                        if i > 0:
                            v.wait_ge(s_rstd, i)
                        x = xbuf[i % 2]
                        v.tensor_reduce(xsum[:], x[:], axis=AXX, op=ALU.add)
                        v.scalar_tensor_tensor(rt1[:], x[:], 1.0, x[:], ALU.mult, ALU.mult,
                                               accum_out=xsq[:])
                        v.drain()
                        v.tensor_scalar(meant[:], xsum[:], 1.0 / 2048.0, None, ALU.mult)
                        v.drain()
                        v.tensor_tensor(m2t[:], meant[:], meant[:], ALU.mult)
                        v.drain()
                        v.scalar_tensor_tensor(vart[:], xsq[:], 1.0 / 2048.0, m2t[:],
                                               ALU.mult, ALU.subtract)
                        v.drain()
                        v.nop().then_inc(s_var, 1)
                        v.wait_ge(s_rstd, i + 1)
                        v.tensor_scalar(nbias[:], meant[:], rstd[:], -1.0, ALU.mult, ALU.mult).then_inc(s_nb, 1)
                        if A1LVL < 2:
                            continue
                        if i > 0 and A1LVL >= 3:
                            v.wait_ge(s_mmk, 2 * i)
                        for g in range(16):
                            n = 20 * i + g + 1
                            v.wait_ge(s_tp, n)
                            v.tensor_copy(xnT[:, 128 * g:128 * (g + 1)],
                                          tp[(n - 1) % 4][:, 0:128]).then_inc(s_ev, 1)
                        if A1LVL < 4:
                            v.nop().then_inc(s_ev, 4)
                            if A1LVL >= 3:
                                if i > 1:
                                    v.wait_ge(s_vsp[i % 2], 16 * (i // 2))
                                v.wait_ge(s_mmk, 2 * i + 2)
                                v.tensor_copy(vstage[i % 2][:], ps_v[:]).then_inc(s_vc, 1)
                            continue
                        v.wait_ge(s_krstd, i + 1)
                        for seg in range(4):
                            v.tensor_scalar(kn[:, 128 * seg:128 * (seg + 1)],
                                            ps_k[:, 128 * seg:128 * (seg + 1)],
                                            krstd[:, seg:seg + 1], None, ALU.mult)
                        v.drain()
                        v.nop().then_inc(s_kn, 1)
                        if i > 0:
                            v.wait_ge(s_tp, 20 * i)
                        kn3 = kn[:].rearrange("p (h d) -> p h d", d=128)
                        kro3 = kro[:].rearrange("p (h d) -> p h d", d=128)
                        csl = costab[:, 64 * i:64 * (i + 1)].unsqueeze(1).broadcast_to([128, 4, 64])
                        ssl = sintab[:, 64 * i:64 * (i + 1)].unsqueeze(1).broadcast_to([128, 4, 64])
                        r13 = rt1[:, 0:256].rearrange("p (h d) -> p h d", d=64)
                        r23 = rt2[:, 0:256].rearrange("p (h d) -> p h d", d=64)
                        v.tensor_tensor(r13, kn3[:, :, 0:64], csl, ALU.mult)
                        v.tensor_tensor(r23, kn3[:, :, 64:128], ssl, ALU.mult)
                        v.drain()
                        r13b = rt1[:, 256:512].rearrange("p (h d) -> p h d", d=64)
                        r23b = rt2[:, 256:512].rearrange("p (h d) -> p h d", d=64)
                        v.tensor_tensor(kro3[:, :, 0:64], r13, r23, ALU.add)
                        v.tensor_tensor(r13b, kn3[:, :, 64:128], csl, ALU.mult)
                        v.tensor_tensor(r23b, kn3[:, :, 0:64], ssl, ALU.mult)
                        v.drain()
                        v.tensor_tensor(kro3[:, :, 64:128], r13b, r23b, ALU.subtract).then_inc(s_kro, 1)
                        if i % 4 == 0 and i > 0:
                            v.wait_ge(s_ksp, 64 * (i // 4))
                        for g in range(4):
                            n = 20 * i + 16 + g + 1
                            v.wait_ge(s_tp, n)
                            col = 512 * g + 128 * (i % 4)
                            v.tensor_copy(ktst[:, col:col + 128],
                                          tp[(n - 1) % 4][:, 0:128]).then_inc(s_ev, 1)
                        if i > 1:
                            v.wait_ge(s_vsp[i % 2], 16 * (i // 2))
                        v.wait_ge(s_mmk, 2 * i + 2)
                        v.tensor_copy(vstage[i % 2][:], ps_v[:]).then_inc(s_vc, 1)

                @blk.scalar
                def _(s):
                    for i in range(NIT):
                        s.wait_ge(s_var, i + 1)
                        s.activation(rstd[:], vart[:], AF.Ln, bias=eps_ln_t[:], scale=1.0)
                        s.drain()
                        s.activation(rstd[:], rstd[:], AF.Exp, scale=-0.5)
                        s.drain()
                        s.nop().then_inc(s_rstd, 1)
                        if A1LVL < 2:
                            continue
                        s.wait_ge(s_nb, i + 1)
                        if i > 0:
                            s.wait_ge(s_tp, 20 * (i - 1) + 16)
                        s.activation(xn[:], xbuf[i % 2][:], AF.Identity, bias=nbias[:],
                                     scale=rstd[:]).then_inc(s_xn, 1)
                        if A1LVL < 4:
                            continue
                        s.wait_ge(s_mmk, 2 * i + 1)
                        for seg in range(4):
                            s.activation(sq_scr[:, 128 * seg:128 * (seg + 1)],
                                         ps_k[:, 128 * seg:128 * (seg + 1)],
                                         AF.Square, accum_out=ksq[:, seg:seg + 1])
                        s.drain()
                        s.activation(krstd[:], ksq[:], AF.Ln, bias=eps_rms_t[:], scale=1.0 / 128.0)
                        s.drain()
                        s.activation(krstd[:], krstd[:], AF.Exp, scale=-0.5)
                        s.drain()
                        s.nop().then_inc(s_krstd, 1)

                @blk.tensor
                def _(t):
                    for i in range(NIT):
                        if A1LVL < 2:
                            continue
                        t.wait_ge(s_xn, i + 1)
                        for g in range(16):
                            n = 20 * i + g + 1
                            if n > 4:
                                t.wait_ge(s_ev, n - 4)
                            t.transpose(tp[(n - 1) % 4][:, 0:128],
                                        xn[:, 128 * g:128 * (g + 1)], idb[:]).then_inc(s_tp, 1)
                        if A1LVL < 3:
                            t.nop().then_inc(s_tp, 4)
                            continue
                        t.wait_ge(s_ev, 20 * i + 16)
                        if i > 0 and A1LVL >= 4:
                            t.wait_ge(s_kn, i)
                        for h in range(15):
                            t.matmul(ps_k[:], xnT[:, 128 * h:128 * (h + 1)],
                                     wqkvT[h][:, 2048:2560],
                                     start=(h == 0), stop=False)
                        t.matmul(ps_k[:], xnT[:, 1920:2048], wqkvT[15][:, 2048:2560],
                                 start=False, stop=True).then_inc(s_mmk, 1)
                        if i > 0:
                            t.wait_ge(s_vc, i)
                        for h in range(15):
                            t.matmul(ps_v[:], xnT[:, 128 * h:128 * (h + 1)],
                                     wqkvT[h][:, 2560:3072],
                                     start=(h == 0), stop=False)
                        t.matmul(ps_v[:], xnT[:, 1920:2048], wqkvT[15][:, 2560:3072],
                                 start=False, stop=True).then_inc(s_mmk, 1)
                        if A1LVL < 4:
                            t.nop().then_inc(s_tp, 4)
                            continue
                        t.wait_ge(s_kro, i + 1)
                        for g in range(4):
                            n = 20 * i + 16 + g + 1
                            if n > 4:
                                t.wait_ge(s_ev, n - 4)
                            t.transpose(tp[(n - 1) % 4][:, 0:128],
                                        kro[:, 128 * g:128 * (g + 1)], idb[:]).then_inc(s_tp, 1)

            with nc.Block() as blk:
                @blk.sync
                def _(sync):
                    if A1LVL >= 3:
                        sync.wait_ge(s_vsp[0], 16 * (NIT // 2))
                        sync.wait_ge(s_vsp[1], 16 * (NIT // 2))
                    if A1LVL >= 4:
                        sync.wait_ge(s_ksp, 64 * (NIT // 4))
                    if upto == "A1":
                        sync.wait_ge(s_p0sp, 16 * NPT)
            emit_a2 = (upto != "A1")
            s_ax2 = [sem("s_ax20"), sem("s_ax21")]
            s_var2 = sem("s_var2")
            s_rstd2 = sem("s_rstd2")
            s_nb2 = sem("s_nb2")
            s_xn2 = sem("s_xn2")
            s_tp2 = sem("s_tp2")
            s_ev2 = sem("s_ev2")
            s_mm2 = sem("s_mm2")
            s_qrstd = sem("s_qrstd")
            s_qn = sem("s_qn")
            s_qro = sem("s_qro")
            s_qsp = sem("s_qsp")
            s_xsp = sem("s_xsp")

            NM2 = nm2 if emit_a2 else 0
            with nc.Block() as blk:
                @blk.sync
                def _(sync):
                    for m in range(NM2):
                        if m >= 2:
                            sync.wait_ge(s_xn2, m - 1)
                        sync.dma_start(xbuf[m % 2][:], x_own[128 * m:128 * (m + 1), :]).then_inc(s_ax2[m % 2], 16)

                @blk.gpsimd
                def _(g):
                    for m in range(NM2):
                        g.wait_ge(s_ev2, 32 * m + 16)
                        dst = scr_XnT[:, :, 128 * m:128 * (m + 1)].rearrange("c h t -> h c t")
                        g.dma_start(dst, xnT[:].rearrange("h (c t) -> h c t", t=128)).then_inc(s_xsp, 16)
                        g.wait_ge(s_ev2, 32 * (m + 1))
                        dstq = scr_Q[m].rearrange("h d q -> d h q")
                        g.dma_start(dstq, qtst[:].rearrange("d (h q) -> d h q", q=128)).then_inc(s_qsp, 16)

                @blk.vector
                def _(v):
                    for m in range(NM2):
                        v.wait_ge(s_ax2[m % 2], 16 * (m // 2 + 1))
                        if m > 0:
                            v.wait_ge(s_rstd2, m)
                        x = xbuf[m % 2]
                        v.tensor_reduce(xsum[:], x[:], axis=AXX, op=ALU.add)
                        v.scalar_tensor_tensor(rt1[:], x[:], 1.0, x[:], ALU.mult, ALU.mult,
                                               accum_out=xsq[:])
                        v.drain()
                        v.tensor_scalar(meant[:], xsum[:], 1.0 / 2048.0, None, ALU.mult)
                        v.drain()
                        v.tensor_tensor(m2t[:], meant[:], meant[:], ALU.mult)
                        v.drain()
                        v.scalar_tensor_tensor(vart[:], xsq[:], 1.0 / 2048.0, m2t[:],
                                               ALU.mult, ALU.subtract)
                        v.drain()
                        v.nop().then_inc(s_var2, 1)
                        v.wait_ge(s_rstd2, m + 1)
                        v.tensor_scalar(nbias[:], meant[:], rstd[:], -1.0, ALU.mult, ALU.mult).then_inc(s_nb2, 1)
                        if m > 0:
                            v.wait_ge(s_xsp, 16 * m)
                            v.wait_ge(s_mm2, 4 * m)
                        for g in range(16):
                            n = 32 * m + g + 1
                            v.wait_ge(s_tp2, n)
                            v.tensor_copy(xnT[:, 128 * g:128 * (g + 1)],
                                          tp[(n - 1) % 4][:, 0:128]).then_inc(s_ev2, 1)
                        v.wait_ge(s_qrstd, m + 1)
                        for seg in range(16):
                            v.tensor_scalar(qn[:, 128 * seg:128 * (seg + 1)],
                                            ps_q[seg // 4][:, 128 * (seg % 4):128 * (seg % 4) + 128],
                                            qrstd[:, seg:seg + 1],
                                            gainb[:, seg:seg + 1], ALU.mult, ALU.mult)
                        v.drain()
                        v.nop().then_inc(s_qn, 4)
                        if m > 0:
                            v.wait_ge(s_tp2, 32 * m)
                        qn3 = qn[:].rearrange("p (h d) -> p h d", d=128)
                        qro3 = qro[:].rearrange("p (h d) -> p h d", d=128)
                        csl = qcostab[:, 64 * m:64 * (m + 1)].unsqueeze(1).broadcast_to([128, 16, 64])
                        ssl = qsintab[:, 64 * m:64 * (m + 1)].unsqueeze(1).broadcast_to([128, 16, 64])
                        r13 = rt1[:, 0:1024].rearrange("p (h d) -> p h d", d=64)
                        r23 = rt2[:, 0:1024].rearrange("p (h d) -> p h d", d=64)
                        r13b = rt1[:, 1024:2048].rearrange("p (h d) -> p h d", d=64)
                        r23b = rt2[:, 1024:2048].rearrange("p (h d) -> p h d", d=64)
                        v.tensor_tensor(r13, qn3[:, :, 0:64], csl, ALU.mult)
                        v.tensor_tensor(r23, qn3[:, :, 64:128], ssl, ALU.mult)
                        v.drain()
                        v.tensor_tensor(qro3[:, :, 0:64], r13, r23, ALU.add)
                        v.tensor_tensor(r13b, qn3[:, :, 64:128], csl, ALU.mult)
                        v.tensor_tensor(r23b, qn3[:, :, 0:64], ssl, ALU.mult)
                        v.drain()
                        v.tensor_tensor(qro3[:, :, 64:128], r13b, r23b, ALU.subtract).then_inc(s_qro, 1)
                        if m > 0:
                            v.wait_ge(s_qsp, 16 * m)
                        for g in range(16):
                            n = 32 * m + 16 + g + 1
                            v.wait_ge(s_tp2, n)
                            v.tensor_copy(qtst[:, 128 * g:128 * (g + 1)],
                                          tp[(n - 1) % 4][:, 0:128]).then_inc(s_ev2, 1)

                @blk.scalar
                def _(s):
                    for m in range(NM2):
                        s.wait_ge(s_var2, m + 1)
                        s.activation(rstd[:], vart[:], AF.Ln, bias=eps_ln_t[:], scale=1.0)
                        s.drain()
                        s.activation(rstd[:], rstd[:], AF.Exp, scale=-0.5)
                        s.drain()
                        s.nop().then_inc(s_rstd2, 1)
                        s.wait_ge(s_nb2, m + 1)
                        if m > 0:
                            s.wait_ge(s_tp2, 32 * (m - 1) + 16)
                        s.activation(xn[:], xbuf[m % 2][:], AF.Identity, bias=nbias[:],
                                     scale=rstd[:]).then_inc(s_xn2, 1)
                        s.wait_ge(s_mm2, 4 * (m + 1))
                        for seg in range(16):
                            if seg > 0 and seg % 4 == 0:
                                s.drain()
                            s.activation(sq_scr[:, 128 * (seg % 4):128 * (seg % 4) + 128],
                                         ps_q[seg // 4][:, 128 * (seg % 4):128 * (seg % 4) + 128],
                                         AF.Square, accum_out=qsq[:, seg:seg + 1])
                        s.drain()
                        s.activation(qrstd[:], qsq[:], AF.Ln, bias=eps_rms_t[:], scale=1.0 / 128.0)
                        s.drain()
                        s.activation(qrstd[:], qrstd[:], AF.Exp, scale=-0.5)
                        s.drain()
                        s.nop().then_inc(s_qrstd, 1)

                @blk.tensor
                def _(t):
                    for m in range(NM2):
                        t.wait_ge(s_xn2, m + 1)
                        for g in range(16):
                            n = 32 * m + g + 1
                            if n > 4:
                                t.wait_ge(s_ev2, n - 4)
                            t.transpose(tp[(n - 1) % 4][:, 0:128],
                                        xn[:, 128 * g:128 * (g + 1)], idb[:]).then_inc(s_tp2, 1)
                        t.wait_ge(s_ev2, 32 * m + 16)
                        if m > 0:
                            t.wait_ge(s_qn, 4 * m)
                        for oc in range(4):
                            for h in range(15):
                                t.matmul(ps_q[oc][:], xnT[:, 128 * h:128 * (h + 1)],
                                         wqkvT[h][:, 512 * oc:512 * (oc + 1)],
                                         start=(h == 0), stop=False)
                            t.matmul(ps_q[oc][:], xnT[:, 1920:2048],
                                     wqkvT[15][:, 512 * oc:512 * (oc + 1)],
                                     start=False, stop=True).then_inc(s_mm2, 1)
                        t.wait_ge(s_qro, m + 1)
                        for g in range(16):
                            n = 32 * m + 16 + g + 1
                            if n > 4:
                                t.wait_ge(s_ev2, n - 4)
                            t.transpose(tp[(n - 1) % 4][:, 0:128],
                                        qro[:, 128 * g:128 * (g + 1)], idb[:]).then_inc(s_tp2, 1)

            with nc.Block() as blk:
                @blk.sync
                def _(sync):
                    sync.wait_ge(s_qsp, 16 * NM2)
                    sync.wait_ge(s_xsp, 16 * NM2)
                    sync.wait_ge(s_p0sp, 16 * NPT)

        wqkv_scope.close()

        if upto in ("A", "A1"):
            return nc

        _emit_phase_bc(nc, root, sem, dict(
            scr_K=scr_K, scr_V=scr_V, scr_Q=scr_Q, scr_XnT=scr_XnT, scr_WpT=scr_WpT,
            w_gate=w_gate, w_up=w_up, w_down=w_down, x_own=x_own, out_own=out_own,
            idb=idb, idf=idf, eps_rms_t=eps_rms_t, dmask_t=dmask_t,
            sg_col=sg_col, su_col=su_col, sd_col=sd_col))
        return nc


def _emit_phase_bc(nc, root, sem, env):
    scr_K, scr_V, scr_Q = env["scr_K"], env["scr_V"], env["scr_Q"]
    scr_XnT, scr_WpT = env["scr_XnT"], env["scr_WpT"]
    w_gate, w_up, w_down = env["w_gate"], env["w_up"], env["w_down"]
    x_own, out_own = env["x_own"], env["out_own"]
    idb, idf = env["idb"], env["idf"]
    eps_rms_t, dmask_t = env["eps_rms_t"], env["dmask_t"]
    sg_col, su_col, sd_col = env["sg_col"], env["su_col"], env["sd_col"]

    pb = ExitStack()
    with pb:
        bsb = lambda name, shape, dt: pb.enter_context(nc.sbuf_tensor(name, shape, dt))
        PT = bsb("b_PT", [128, 32768], BF16)
        scrF = bsb("b_S", [128, 8192], F32)
        Pb = bsb("b_P", [128, 8192], BF16)
        shB = bsb("b_shB", [128, 16384], BF16)
        shC = bsb("b_shC", [128, 16384], BF16)
        xnT_o = bsb("b_xnT", [128, 8192], BF16)
        qsb2 = bsb("b_q2", [128, 1024], BF16)
        qsb = [qsb2[:, 0:512], qsb2[:, 512:1024]]
        sT_hc = qsb2[:].bitcast(F32)
        wpof_b = None
        gTsil = bsb("b_gts", [128, 512], F32)
        sil = gTsil
        gT = gTsil
        ystage = bsb("b_yst", [128, 512], BF16)
        diagb = None
        xres_b = bsb("b_xr0", [128, 512], F32)
        xres = [xres_b, xres_b]
        ostage = bsb("b_ost", [128, 512], F32)
        ncm = bsb("b_ncm", [128, 32], F32)
        dens = bsb("b_dens", [128, 32], F32)
        dlt = bsb("b_dlt", [128, 32], F32)
        edt = bsb("b_edt", [128, 32], F32)
        sct = bsb("b_sct", [128, 32], F32)
        nm_fin = bsb("b_nmf", [128, 1], F32)
        den_tot = bsb("b_dent", [128, 1], F32)
        rden = bsb("b_rden", [128, 1], F32)
        ssq = bsb("b_ssq", [128, 1], F32)
        ssq_p = bsb("b_ssqp", [128, 4], F32)
        yrstd = bsb("b_yrstd", [128, 1], F32)
        rms_scr = ystage

        ps_s = [pb.enter_context(nc.psum_tensor(f"b_pss{i}", [128, 512], F32))
                for i in range(3)]
        tpg = [pb.enter_context(nc.psum_tensor(f"b_tp{i}", [128, 2048], BF16))
               for i in range(2)]
        ps_acc = pb.enter_context(nc.psum_tensor("b_acc", [128, 512], F32))

        names = ["b_qld", "b_kv", "b_smm", "b_pexp", "b_cm", "b_sm1", "b_ed", "b_pn", "b_dg",
                 "b_ptp", "b_ptev", "b_pv", "b_yev", "b_rsq", "b_yrstd", "b_yrs",
                 "b_wg", "b_gum", "b_sil", "b_fu", "b_wp", "b_wd", "b_xr",
                 "b_pmm", "b_sev", "b_dmm", "b_gsil", "b_z", "b_out", "b_pz"]
        SM = {n: sem(n) for n in names}
        (b_qld, b_kv, b_smm, b_pexp, b_cm, b_sm1, b_ed, b_pn, b_dg, b_ptp, b_ptev, b_pv,
         b_yev, b_rsq, b_yrstd, b_yrs, b_wg, b_gum, b_sil, b_fu, b_wp, b_wd,
         b_xr, b_pmm, b_sev, b_dmm, b_gsil, b_z, b_out, b_pz) = (SM[n] for n in names)
        b_qld2 = [sem("b_qld0"), sem("b_qld1")]
        b_wg2 = [sem("b_wg20"), sem("b_wg21")]
        b_wd2 = [sem("b_wd20"), sem("b_wd21")]
        b_xnt = sem("b_xnt")

        wgbuf = [shB[:, 0:8192], shC[:, 8192:16384]]
        wubuf = [shC[:, 0:8192], scrF[:].bitcast(BF16)[:, 0:8192]]
        wdbuf = [scrF[:].bitcast(BF16)[:, 8192:16384], Pb[:, 0:8192]]
        wpof_b = scrF[:].bitcast(BF16)[:, 0:2048]
        diagb = scrF[:].bitcast(BF16)[:, 0:4096]
        Ksb = shC[:, 0:8192]
        Vtile = lambda ks: shC[:, 8192 + 128 * ks: 8192 + 128 * (ks + 1)]
        PTtile = lambda ks: PT[:, 512 * ks:512 * (ks + 1)]
        PT3 = PT[:].rearrange("p (n q) -> p n q", q=512)
        fuseTt = lambda osub: PT[:, 512 * osub:512 * (osub + 1)]
        ybufq = lambda qs: shB[:, 2048 * qs:2048 * (qs + 1)]
        yb3 = shB[:, 0:8192].rearrange("p (q c) -> p q c", q=4)
        yrTt = lambda c: shB[:, 8192 + 512 * c:8192 + 512 * (c + 1)]
        yr3 = shB[:, 8192:16384].rearrange("p (c t) -> p c t", c=16)
        xn3 = lambda c: xnT_o[:, 512 * c:512 * (c + 1)]
        ost3 = ostage[:].rearrange("p (a h) -> p a h", a=4)

        def tp_slot(idx):
            return tpg[(idx // 2) % 2][:, 1024 * (idx % 2):1024 * (idx % 2) + 128]

        def tp_slot_f32(idx):
            return tpg[(idx // 2) % 2][:].bitcast(F32)[:, 512 * (idx % 2):512 * (idx % 2) + 128]

        def tp_pair(idx):
            return tpg[(idx // 2) % 2][:].rearrange("p (s c) -> p s c", s=2)[:, :, 0:128]

        def tp_pair_f32(idx):
            return tpg[(idx // 2) % 2][:].bitcast(F32).rearrange("p (s c) -> p s c", s=2)[:, :, 0:128]

        heads = [(t, g, hq) for t in range(4) for g in range(4) for hq in range(4)]
        nchunk = lambda t, qs: 4 * t + qs + 1

        smm_at = {}
        acc = 0
        for hidx, (t, g, hq) in enumerate(heads):
            for qs in range(4):
                smm_at[(hidx, qs)] = acc
                acc += nchunk(t, qs)
        head_smm_end = {h: smm_at[(h, 3)] + nchunk(heads[h][0], 3) for h in range(64)}
        SMM_TOT = acc

        tr_at = {}
        tr = 0
        for t in range(4):
            for j in range(16):
                hidx = 16 * t + j
                for qs in range(4):
                    tr_at[("P", hidx, qs)] = tr
                    tr += 4 * nchunk(t, qs)
                tr_at[("Pend", hidx)] = tr
                if j >= 1:
                    tr_at[("Yend", hidx - 1)] = tr
                    tr += 4
            tr_at[("Yend", 16 * t + 15)] = tr
            tr += 4
            for qs in range(4):
                tr_at[("YR", t, qs)] = tr
                tr += 16
            for hc in range(16):
                tr_at[("Z", t, hc)] = tr
                tr += 4
        TR_TOT = tr
        trP_end = lambda h: tr_at[("Pend", h)]
        trY_end = lambda h: tr_at[("Yend", h)] + 4
        trYR_end = lambda t: tr_at[("YR", t, 3)] + 16
        trZ_end = lambda t, hc: tr_at[("Z", t, hc)] + 4

        aq = lambda hidx, qs: 4 * hidx + qs

        with nc.Block() as blk:
            @blk.sync
            def _(sync):
                for t in range(4):
                    kvt = (4 * t + 4) * 512
                    if t > 0 and 4 <= BCLVL < 11:
                        sync.wait_ge(b_gum, 128 * t)
                    for g in range(4):
                        gi = 4 * t + g
                        if gi > 0:
                            prev_last = 4 * gi - 1
                            sync.wait_ge(b_smm, head_smm_end[prev_last])
                            if 2 <= BCLVL < 11:
                                sync.wait_ge(b_pv, prev_last + 1)
                            else:
                                sync.wait_ge(b_ptev, trP_end(prev_last) // 2)
                        sync.dma_start(Ksb[:, 0:kvt], scr_K[g][:, 0:kvt]).then_inc(b_kv, 16)
                        sync.dma_start(shC[:, 8192:16384].rearrange("p (n d) -> p n d", d=128)[:, 0:kvt // 128, :],
                                       scr_V[0:kvt, 128 * g:128 * (g + 1)]
                                       .rearrange("(n p) d -> p n d", p=128)).then_inc(b_kv, 16)
                        for hq in range(4):
                            hidx = 4 * gi + hq
                            if hidx >= 2:
                                sync.wait_ge(b_smm, head_smm_end[hidx - 2])
                            if t > 0 and hidx % 16 < 2 and 5 <= BCLVL < 11:
                                sync.wait_ge(b_z, 16 * t)
                            src = scr_Q[4 * t:4 * t + 4, 4 * g + hq].rearrange("m d q -> d m q")
                            sync.dma_start(qsb[hidx % 2].rearrange("d (m q) -> d m q", q=128),
                                           src).then_inc(b_qld2[hidx % 2], 16)
                    if BCLVL < 4 or BCLVL >= 11:
                        continue
                    if t > 0:
                        sync.wait_ge(b_gum, 128 * t)
                    sync.dma_start(xnT_o[:].rearrange("h (c t2) -> h c t2", c=16),
                                   scr_XnT[:, :, 512 * t:512 * (t + 1)]
                                   .rearrange("c h t2 -> h c t2")).then_inc(b_xnt, 16)
                    for hc in range(16):
                        if BCLVL < 5 or BCLVL >= 11:
                            continue
                        pidx = 16 * t + hc
                        sync.wait_ge(b_gum, 128 * (t + 1))
                        if pidx >= 1:
                            sync.wait_ge(b_pmm, pidx)
                        sync.dma_start(wpof_b.rearrange("h (c o) -> h c o", c=16),
                                       scr_WpT[:, :, 128 * hc:128 * (hc + 1)]
                                       .rearrange("c h o -> h c o")).then_inc(b_wp, 16)
                        if pidx >= 1:
                            pt_, ph_ = divmod(pidx - 1, 16)
                            sync.wait_ge(b_ptev, trZ_end(pt_, ph_) // 2)
                        sync.dma_start(xres_b[:].rearrange("p (a h2) -> p a h2", a=4),
                                       x_own[512 * t:512 * (t + 1), 128 * hc:128 * (hc + 1)]
                                       .rearrange("(a p) h2 -> p a h2", p=128)).then_inc(b_xr, 16)
                        sync.wait_ge(b_ptev, trZ_end(t, hc) // 2)
                        sync.dma_start(out_own[512 * t:512 * (t + 1), 128 * hc:128 * (hc + 1)]
                                       .rearrange("(a p) h2 -> p a h2", p=128),
                                       ost3).then_inc(b_out, 16)
                if 5 <= BCLVL < 11:
                    sync.wait_ge(b_out, 16 * 64)
                elif 2 <= BCLVL < 11:
                    sync.wait_ge(b_pv, 64)
                else:
                    sync.wait_ge(b_ptev, TR_TOT // 2)

            @blk.gpsimd
            def _(g_):
                for t in range(4):
                    if BCLVL < 2 or BCLVL in (11, 12, 13, 14):
                        g_.nop().then_inc(b_pz, 1)
                        continue
                    if 5 <= BCLVL < 11:
                        g_.wait_ge(b_dmm, 16 * t)
                    g_.wait_ge(b_pv, 16 * t)
                    for qs in range(2):
                        nks = 12 - 4 * qs
                        ks0 = 4 * (4 * t + qs + 1)
                        g_.memset(PT3[:, ks0:ks0 + nks, 128 * qs:128 * (qs + 1)], 0.0)
                    g_.memset(PT3[:, 4 * (4 * t + 3):4 * (4 * t + 3) + 4, 256:384],
                              0.0).then_inc(b_pz, 1)
                    if BCLVL < 4 or BCLVL >= 11:
                        continue
                    g_.wait_ge(b_ptp, trYR_end(t))
                    g_.wait_ge(b_pv, 16 * (t + 1))
                    g_.wait_ge(b_smm, head_smm_end[16 * t + 15])
                    g_.wait_ge(b_pexp, head_smm_end[16 * t + 15])
                    if t > 0:
                        g_.wait_ge(b_pmm, 16 * t)
                    for oc in range(16):
                        if oc >= 2:
                            g_.wait_ge(b_gum, 128 * t + 8 * (oc - 1))
                        g_.dma_start(wgbuf[oc % 2].rearrange("p (c o) -> p c o", c=16),
                                     w_gate[:, 512 * oc:512 * (oc + 1)]
                                     .rearrange("(c p) o -> p c o", p=128)).then_inc(b_wg2[(16 * t + oc) % 2], 16)
                        g_.dma_start(wubuf[oc % 2].rearrange("p (c o) -> p c o", c=16),
                                     w_up[:, 512 * oc:512 * (oc + 1)]
                                     .rearrange("(c p) o -> p c o", p=128)).then_inc(b_wg2[(16 * t + oc) % 2], 16)
                    if BCLVL < 5 or BCLVL >= 11:
                        continue
                    g_.wait_ge(b_ptp, trP_end(16 * t + 15))
                    for hc in range(16):
                        pidx = 16 * t + hc
                        if pidx >= 2:
                            g_.wait_ge(b_dmm, pidx - 1)
                        g_.dma_start(wdbuf[hc % 2].rearrange("p (n h) -> p n h", n=64),
                                     w_down[:, 128 * hc:128 * (hc + 1)]
                                     .rearrange("(n p) h -> p n h", p=128)).then_inc(b_wd2[pidx % 2], 16)

            @blk.tensor
            def _(t_):
                def emit_S_pre(hidx):
                    t, g, hq = heads[hidx]
                    if hidx % 16 == 0 and 4 <= BCLVL < 11:
                        t_.wait_ge(b_sil, 64 * t)
                        t_.wait_ge(b_fu, 64 * t)
                    if hq == 0:
                        t_.wait_ge(b_kv, 16 * (2 * (4 * t + g) + 2))
                    t_.wait_ge(b_qld2[hidx % 2], 16 * (hidx // 2 + 1))

                def emit_S_chunks(hidx, qs, k0, k1):
                    t, g, hq = heads[hidx]
                    base = smm_at[(hidx, qs)]
                    for kc in range(k0, k1):
                        idx = base + kc
                        if idx >= 3:
                            t_.wait_ge(b_pexp, idx - 2)
                        t_.matmul(ps_s[idx % 3][:],
                                  qsb2[:, 512 * (hidx % 2) + 128 * qs:512 * (hidx % 2) + 128 * (qs + 1)],
                                  Ksb[:, 512 * kc:512 * (kc + 1)],
                                  start=True, stop=True).then_inc(b_smm, 1)

                def emit_tpP_qs(hidx, qs, half=None):
                    t, g, hq = heads[hidx]
                    n = nchunk(t, qs)
                    if BCLVL in (11, 12, 13, 14):
                        if half != 1:
                            t_.nop().then_inc(b_ptp, 4 * n)
                        return
                    c1h = (n + 1) // 2
                    j_lo, j_hi = 0, 4 * n
                    if half == 0:
                        j_hi = 4 * c1h
                    elif half == 1:
                        j_lo = 4 * c1h
                    base = smm_at[(hidx, qs)]
                    tr0 = tr_at[("P", hidx, qs)]
                    for j in range(j_lo, j_hi):
                        idx = tr0 + j
                        if j % 4 == 0:
                            t_.wait_ge(b_dg, base + j // 4 + 1)
                        if idx >= 4:
                            t_.wait_ge(b_ptev, (idx - 4) // 2 + 1)
                        dslot = 16 * (qs % 2) + j // 4
                        t_.matmul(tp_slot_f32(idx), Pb[:, 128 * j:128 * (j + 1)],
                                  diagb[:, 128 * dslot:128 * (dslot + 1)],
                                  start=True, stop=True).then_inc(b_ptp, 1)

                def emit_PV(hidx):
                    if BCLVL < 2 or BCLVL in (11, 12, 13, 14):
                        t_.nop().then_inc(b_pv, 1)
                        t_.nop().then_inc(b_ptp, 4)
                        return
                    t, g, hq = heads[hidx]
                    nks = (t + 1) * 16
                    p30 = tr_at[("P", hidx, 3)] // 2
                    t_.wait_ge(b_pz, t + 1)
                    if hidx % 16 == 0 and 5 <= BCLVL < 11:
                        t_.wait_ge(b_gsil, 16 * t)
                    if hidx > 0:
                        t_.wait_ge(b_yev, hidx)
                    for ks in range(nks - 1):
                        if ks % 2 == 0:
                            t_.wait_ge(b_ptev, p30 + ks // 2 + 1)
                        t_.matmul(ps_acc[:], Vtile(ks), PTtile(ks),
                                  start=(ks == 0), stop=False)
                    t_.matmul(ps_acc[:], Vtile(nks - 1), PTtile(nks - 1),
                              start=(nks == 1), stop=True).then_inc(b_pv, 1)

                def emit_ytr(hidx):
                    if BCLVL < 2 or BCLVL in (11, 12, 13, 14):
                        return
                    t_.wait_ge(b_yev, hidx + 1)
                    tr0 = tr_at[("Yend", hidx)]
                    for j in range(4):
                        idx = tr0 + j
                        if idx >= 4:
                            t_.wait_ge(b_ptev, (idx - 4) // 2 + 1)
                        t_.transpose(tp_slot(idx), ystage[:, 128 * j:128 * (j + 1)],
                                     idb[:]).then_inc(b_ptp, 1)

                for t in range(4):
                    for j in range(16):
                        h = 16 * t + j
                        if j >= 1:
                            emit_PV(h - 1)
                        emit_S_pre(h)
                        for qs in range(4):
                            n = nchunk(t, qs)
                            k1 = min(3, n)
                            k2 = min(6, n)
                            emit_S_chunks(h, qs, 0, k1)
                            if qs >= 1:
                                emit_tpP_qs(h, qs - 1, half=0)
                            emit_S_chunks(h, qs, k1, k2)
                            if qs >= 1:
                                emit_tpP_qs(h, qs - 1, half=1)
                            emit_S_chunks(h, qs, k2, n)
                        emit_tpP_qs(h, 3)
                        if j >= 1:
                            emit_ytr(h - 1)
                    emit_PV(16 * t + 15)
                    emit_ytr(16 * t + 15)
                    if BCLVL < 3 or BCLVL >= 11:
                        t_.nop().then_inc(b_ptp, 64)
                    for qs in range(4 if 3 <= BCLVL < 11 else 0):
                        t_.wait_ge(b_yrs, 4 * t + qs + 1)
                        tr0 = tr_at[("YR", t, qs)]
                        for j in range(16):
                            idx = tr0 + j
                            if idx >= 4:
                                t_.wait_ge(b_ptev, (idx - 4) // 2 + 1)
                            t_.transpose(tp_slot(idx),
                                         ybufq(qs)[:, 128 * j:128 * (j + 1)],
                                         idb[:]).then_inc(b_ptp, 1)
                    if BCLVL < 4 or BCLVL >= 11:
                        t_.nop().then_inc(b_ptp, 64)
                        t_.nop().then_inc(b_gum, 128)
                        continue
                    t_.wait_ge(b_xnt, 16 * (t + 1))
                    t_.wait_ge(b_pexp, head_smm_end[16 * t + 15])
                    for oc in range(16):
                        gidx = 16 * t + oc
                        t_.wait_ge(b_wg2[gidx % 2], 32 * (gidx // 2 + 1))
                        for osub in range(4):
                            o_lin = 64 * t + 4 * oc + osub
                            if o_lin > 0:
                                t_.wait_ge(b_sil, o_lin)
                            for c in range(15):
                                t_.matmul(ps_s[0][:],
                                          wgbuf[oc % 2][:, 512 * c + 128 * osub:512 * c + 128 * (osub + 1)],
                                          xn3(c), start=(c == 0), stop=False)
                            t_.matmul(ps_s[0][:],
                                      wgbuf[oc % 2][:, 512 * 15 + 128 * osub:512 * 15 + 128 * (osub + 1)],
                                      xn3(15), start=False, stop=True).then_inc(b_gum, 1)
                            if o_lin > 0:
                                t_.wait_ge(b_fu, o_lin)
                            for c in range(15):
                                t_.matmul(ps_s[1][:],
                                          wubuf[oc % 2][:, 512 * c + 128 * osub:512 * c + 128 * (osub + 1)],
                                          xn3(c), start=(c == 0), stop=False)
                            t_.matmul(ps_s[1][:],
                                      wubuf[oc % 2][:, 512 * 15 + 128 * osub:512 * 15 + 128 * (osub + 1)],
                                      xn3(15), start=False, stop=True).then_inc(b_gum, 1)
                    if BCLVL < 5 or BCLVL >= 11:
                        t_.nop().then_inc(b_ptp, 64)
                        t_.nop().then_inc(b_pmm, 16)
                        t_.nop().then_inc(b_dmm, 16)
                        continue
                    t_.wait_ge(b_ptev, trYR_end(t) // 2)
                    t_.wait_ge(b_yev, 16 * (t + 1))
                    for hc in range(16):
                        pidx = 16 * t + hc
                        t_.wait_ge(b_wp, 16 * (pidx + 1))
                        if pidx > 0:
                            t_.wait_ge(b_gsil, pidx)
                        for c in range(15):
                            t_.matmul(ps_acc[:], wpof_b[:, 128 * c:128 * (c + 1)],
                                      yrTt(c), start=(c == 0), stop=False)
                        t_.matmul(ps_acc[:], wpof_b[:, 1920:2048], yrTt(15),
                                  start=False, stop=True).then_inc(b_pmm, 1)
                        t_.wait_ge(b_sev, pidx + 1)
                        t_.wait_ge(b_fu, 64 * (t + 1))
                        t_.wait_ge(b_wd2[pidx % 2], 16 * (pidx // 2 + 1))
                        for osub in range(63):
                            t_.matmul(ps_acc[:], wdbuf[hc % 2][:, 128 * osub:128 * (osub + 1)],
                                      fuseTt(osub), start=(osub == 0), stop=False)
                        t_.matmul(ps_acc[:], wdbuf[hc % 2][:, 8064:8192], fuseTt(63),
                                  start=False, stop=True).then_inc(b_dmm, 1)
                        t_.wait_ge(b_z, pidx + 1)
                        tr0 = tr_at[("Z", t, hc)]
                        for j in range(4):
                            idx = tr0 + j
                            if idx >= 4:
                                t_.wait_ge(b_ptev, (idx - 4) // 2 + 1)
                            t_.transpose(tp_slot_f32(idx), gT[:, 128 * j:128 * (j + 1)],
                                         idf[:]).then_inc(b_ptp, 1)

            @blk.vector
            def _(v):
                def emit_ydve(hy):
                    v.wait_ge(b_pv, hy + 1)
                    if hy > 0:
                        v.wait_ge(b_ptp, trY_end(hy - 1))
                    v.tensor_copy(ystage[:], ps_acc[:]).then_inc(b_yev, 1)
                    tr0 = tr_at[("Yend", hy)]
                    hh = 4 * heads[hy][1] + heads[hy][2]
                    hb = 8 + 4 * (hy % 2)
                    for j in range(4):
                        idx = tr0 + j
                        v.wait_ge(b_ptp, tr0 + 2 * (j // 2) + 2)
                        op = v.tensor_scalar(ybufq(j)[:, 128 * hh:128 * (hh + 1)],
                                             tp_slot(idx), sct[:, hb + j:hb + j + 1],
                                             None, ALU.mult)
                        if j % 2 == 1:
                            op.then_inc(b_ptev, 1)

                for t in range(4):
                    for j in range(16):
                        hidx = 16 * t + j
                        pend = []
                        for qs in range(4):
                            n = nchunk(t, qs)
                            m = 4 * t + qs
                            base = smm_at[(hidx, qs)]
                            par = 16 * (qs % 2)
                            a = aq(hidx, qs)
                            if hidx % 16 == 0 and qs == 0 and 4 <= BCLVL < 11:
                                v.wait_ge(b_gum, 128 * t)
                                if BCLVL >= 5:
                                    v.wait_ge(b_dmm, 16 * t)
                            if a >= 2:
                                v.wait_ge(b_pn, a - 1)
                            for kc in range(n):
                                idx = base + kc
                                v.wait_ge(b_smm, idx + 1)
                                if kc == m:
                                    v.scalar_tensor_tensor(ps_s[idx % 3][:], ps_s[idx % 3][:],
                                                           1.0, dmask_t[:], ALU.mult, ALU.add)
                                    v.drain()
                                v.reduce_max(ncm[:, par + kc:par + kc + 1], ps_s[idx % 3][:],
                                             axis=AXX, negate=True).then_inc(b_cm, 1)
                                if kc == 2:
                                    n_prev = len(pend) // 2 + len(pend) % 2
                                    while len(pend) > n_prev:
                                        pend.pop(0)()
                                elif kc >= 5:
                                    while pend:
                                        pend.pop(0)()
                                else:
                                    for _ in range(2):
                                        if pend:
                                            pend.pop(0)()
                            while pend:
                                pend.pop(0)()
                            v.wait_ge(b_pexp, base + n)
                            v.drain()
                            v.tensor_reduce(nm_fin[:], ncm[:, par:par + n], axis=AXX, op=ALU.min)
                            v.drain()
                            v.tensor_scalar(dlt[:, par:par + n], ncm[:, par:par + n],
                                            nm_fin[:], None, ALU.subtract).then_inc(b_sm1, 1)
                            v.wait_ge(b_ed, a + 1)
                            dg0 = 16 * (qs % 2)
                            for kc in range(n):
                                v.tensor_scalar(diagb[:, 128 * (dg0 + kc):128 * (dg0 + kc + 1)],
                                                idb[:],
                                                edt[:, par + kc:par + kc + 1], None,
                                                ALU.mult).then_inc(b_dg, 1)
                            v.scalar_tensor_tensor(sct[:, 16:16 + n], dens[:, par:par + n],
                                                   1.0, edt[:, par:par + n], ALU.mult, ALU.mult,
                                                   accum_out=sct[:, 4 * (hidx % 2) + qs:
                                                                4 * (hidx % 2) + qs + 1])
                            v.drain()
                            v.nop().then_inc(b_pn, 1)
                            if qs == 0:
                                v.wait_ge(b_pv, hidx)
                                if hidx % 16 == 0:
                                    v.wait_ge(b_pz, t + 1)
                                    if 5 <= BCLVL < 11:
                                        v.wait_ge(b_dmm, 16 * t)
                            tr0 = tr_at[("P", hidx, qs)]

                            def mk_evac(idx_, j2_, qs_):
                                def f():
                                    v.wait_ge(b_ptp, idx_ + 2)
                                    v.tensor_copy(PT3[:, 2 * j2_:2 * j2_ + 2,
                                                      128 * qs_:128 * (qs_ + 1)],
                                                  tp_pair_f32(idx_)).then_inc(b_ptev, 1)
                                return f

                            evs = [mk_evac(tr0 + 2 * j2, j2, qs) for j2 in range(2 * n)]
                            if qs == 3:
                                for f in evs:
                                    f()
                                hb0 = 4 * (hidx % 2)
                                v.drain()
                                for q_ in range(4):
                                    v.reciprocal(sct[:, 8 + hb0 + q_:8 + hb0 + q_ + 1],
                                                 sct[:, hb0 + q_:hb0 + q_ + 1])
                                    v.drain()
                            else:
                                pend = evs
                        if BCLVL < 2 or BCLVL in (11, 12, 13, 14):
                            v.nop().then_inc(b_yev, 1)
                            v.nop().then_inc(b_ptev, 2)
                            continue
                        if j >= 1:
                            emit_ydve(hidx - 1)
                    if not (BCLVL < 2 or BCLVL in (11, 12, 13, 14)):
                        emit_ydve(16 * t + 15)
                    if BCLVL < 3 or BCLVL >= 11:
                        v.nop().then_inc(b_rsq, 4)
                        v.nop().then_inc(b_yrs, 4)
                        v.nop().then_inc(b_ptev, 32)
                        v.nop().then_inc(b_fu, 64)
                        v.nop().then_inc(b_sev, 16)
                        v.nop().then_inc(b_z, 16)
                        v.nop().then_inc(b_ptev, 32)
                        continue
                    v.wait_ge(b_ptp, trY_end(16 * t + 15))
                    v.drain()
                    for qs in range(4):
                        yq = ybufq(qs)
                        for q4 in range(4):
                            if q4 > 0:
                                v.drain()
                            v.scalar_tensor_tensor(rms_scr[:], yq[:, 512 * q4:512 * (q4 + 1)],
                                                   1.0, yq[:, 512 * q4:512 * (q4 + 1)],
                                                   ALU.mult, ALU.mult,
                                                   accum_out=ssq_p[:, q4:q4 + 1])
                        v.drain()
                        v.tensor_reduce(ssq[:], ssq_p[:], axis=AXX, op=ALU.add).then_inc(b_rsq, 1)
                        v.wait_ge(b_yrstd, 4 * t + qs + 1)
                        v.drain()
                        v.tensor_scalar(yq[:], yq[:], yrstd[:], None, ALU.mult).then_inc(b_yrs, 1)
                        if qs == 0 and t > 0:
                            v.wait_ge(b_pmm, 16 * t)
                        tr0 = tr_at[("YR", t, qs)]
                        for j2 in range(8):
                            idx = tr0 + 2 * j2
                            v.wait_ge(b_ptp, idx + 2)
                            v.tensor_copy(yr3[:, 2 * j2:2 * j2 + 2, 128 * qs:128 * (qs + 1)],
                                          tp_pair(idx)).then_inc(b_ptev, 1)
                    if BCLVL < 4 or BCLVL >= 11:
                        v.nop().then_inc(b_fu, 64)
                        v.nop().then_inc(b_sev, 16)
                        v.nop().then_inc(b_z, 16)
                        v.nop().then_inc(b_ptev, 32)
                        continue
                    v.wait_ge(b_pv, 16 * (t + 1))
                    for oc in range(16):
                        for osub in range(4):
                            o = 4 * oc + osub
                            o_lin = 64 * t + o
                            v.wait_ge(b_sil, o_lin + 1)
                            v.wait_ge(b_gum, 128 * t + 8 * oc + 2 * osub + 2)
                            v.scalar_tensor_tensor(fuseTt(o), ps_s[1][:],
                                                   su_col[:, o:o + 1], sil[:],
                                                   ALU.mult, ALU.mult).then_inc(b_fu, 1)
                    if BCLVL < 5 or BCLVL >= 11:
                        if 4 <= BCLVL < 11:
                            v.nop().then_inc(b_sev, 16)
                            v.nop().then_inc(b_z, 16)
                            v.nop().then_inc(b_ptev, 32)
                        continue
                    for hc in range(16):
                        pidx = 16 * t + hc
                        v.wait_ge(b_pmm, pidx + 1)
                        v.tensor_copy(sT_hc, ps_acc[:]).then_inc(b_sev, 1)
                        v.wait_ge(b_gsil, pidx + 1)
                        v.drain()
                        v.tensor_tensor(gT[:], gT[:], sT_hc, ALU.mult).then_inc(b_z, 1)
                        v.wait_ge(b_xr, 16 * (pidx + 1))
                        if pidx > 0:
                            v.wait_ge(b_out, 16 * pidx)
                        tr0 = tr_at[("Z", t, hc)]
                        for j2 in range(2):
                            idx = tr0 + 2 * j2
                            v.wait_ge(b_ptp, idx + 2)
                            v.tensor_tensor(ost3[:, 2 * j2:2 * j2 + 2, :],
                                            tp_pair_f32(idx),
                                            xres_b[:].rearrange("p (a h2) -> p a h2", a=4)[:, 2 * j2:2 * j2 + 2, :],
                                            ALU.add).then_inc(b_ptev, 1)

            @blk.scalar
            def _(s):
                for t in range(4):
                    for j in range(16):
                        hidx = 16 * t + j
                        for qs in range(4):
                            n = nchunk(t, qs)
                            base = smm_at[(hidx, qs)]
                            par = 16 * (qs % 2)
                            a = aq(hidx, qs)
                            if hidx % 16 == 0 and qs == 0 and 4 <= BCLVL < 11:
                                s.wait_ge(b_gum, 128 * t)
                                if 5 <= BCLVL < 11:
                                    s.wait_ge(b_dmm, 16 * t)
                            if qs > 0:
                                tr0p, npv = tr_at[("P", hidx, qs - 1)], nchunk(t, qs - 1)
                            elif hidx > 0:
                                tr0p, npv = tr_at[("P", hidx - 1, 3)], nchunk(heads[hidx - 1][0], 3)
                            else:
                                tr0p, npv = None, 0
                            if a >= 2:
                                s.wait_ge(b_pn, a - 1)
                            for kc in range(n):
                                idx = base + kc
                                if tr0p is not None:
                                    s.wait_ge(b_ptp, tr0p + 4 * min(kc + 1, npv))
                                s.wait_ge(b_cm, idx + 1)
                                s.activation(Pb[:, 512 * kc:512 * (kc + 1)], ps_s[idx % 3][:],
                                             AF.Exp, bias=ncm[:, par + kc:par + kc + 1],
                                             scale=1.0,
                                             accum_out=dens[:, par + kc:par + kc + 1]
                                             ).then_inc(b_pexp, 1)
                            s.wait_ge(b_sm1, a + 1)
                            s.drain()
                            s.activation(edt[:, par:par + n], dlt[:, par:par + n], AF.Exp,
                                         scale=-1.0).then_inc(b_ed, 1)
                            s.drain()
                    if BCLVL < 3 or BCLVL >= 11:
                        s.nop().then_inc(b_yrstd, 4)
                    for qs in range(4 if 3 <= BCLVL < 11 else 0):
                        s.wait_ge(b_rsq, 4 * t + qs + 1)
                        s.activation(yrstd[:], ssq[:], AF.Ln, bias=eps_rms_t[:],
                                     scale=1.0 / 2048.0)
                        s.drain()
                        s.activation(yrstd[:], yrstd[:], AF.Exp, scale=-0.5)
                        s.drain()
                        s.nop().then_inc(b_yrstd, 1)
                    if BCLVL < 4 or BCLVL >= 11:
                        s.nop().then_inc(b_sil, 64)
                        s.nop().then_inc(b_gsil, 16)
                    for oc in range(16 if 4 <= BCLVL < 11 else 0):
                        for osub in range(4):
                            o = 4 * oc + osub
                            o_lin = 64 * t + o
                            s.wait_ge(b_gum, 128 * t + 8 * oc + 2 * osub + 1)
                            if o_lin > 0:
                                s.wait_ge(b_fu, o_lin)
                            if oc == 0 and osub == 0 and t > 0:
                                s.wait_ge(b_z, 16 * t)
                                s.wait_ge(b_ptp, trZ_end(t - 1, 15))
                            s.activation(sil[:], ps_s[0][:], AF.Silu,
                                         scale=sg_col[:, o:o + 1]).then_inc(b_sil, 1)
                    if 4 <= BCLVL < 5:
                        s.nop().then_inc(b_gsil, 16)
                    for hc in range(16 if 5 <= BCLVL < 11 else 0):
                        pidx = 16 * t + hc
                        s.wait_ge(b_dmm, pidx + 1)
                        if pidx > 0:
                            s.wait_ge(b_z, pidx)
                            s.wait_ge(b_ptp, trZ_end(*divmod(pidx - 1, 16)))
                        s.activation(gT[:], ps_acc[:], AF.Silu,
                                     scale=sd_col[:, hc:hc + 1]).then_inc(b_gsil, 1)



def _host_tables():
    inv_freq = (1.0 / 10000.0 ** (np.arange(0, HD, 2, dtype=np.float32) / HD)).astype(np.float32)
    t = np.arange(S, dtype=np.float32)
    freqs = t[:, None] * inv_freq[None, :]
    cos = np.cos(freqs.astype(np.float64)).astype(np.float32)
    sin = np.sin(freqs.astype(np.float64)).astype(np.float32)
    return cos, sin


def _own_rows(r):
    idx = []
    for m in range(NM):
        g0 = (r + 4 * m) * 128
        idx.extend(range(g0, g0 + 128))
    return np.asarray(idx)


def _diag_mask(r):
    j = np.arange(512)[None, :]
    i = np.arange(128)[:, None]
    return np.where(j <= r * 128 + i, 0.0, NEG_BIG).astype(np.float32)


def make_in_maps(x, w_qkv, w_proj, q_gain, w_gate, w_up, w_down, s_gate, s_up, s_down):
    cos, sin = _host_tables()
    in_maps = []
    for c in range(8):
        b, r = c // 4, c % 4
        rows = _own_rows(r)
        in_maps.append(dict(
            xb=np.ascontiguousarray(x[b]),
            x_own=np.ascontiguousarray(x[b][rows]),
            w_qkv=np.ascontiguousarray(np.asarray(w_qkv, np.float32)),
            w_proj=np.ascontiguousarray(np.asarray(w_proj, np.float32)),
            q_gain=np.ascontiguousarray(np.asarray(q_gain, np.float32)),
            w_gate=np.ascontiguousarray(np.asarray(w_gate, np.int8)),
            w_up=np.ascontiguousarray(np.asarray(w_up, np.int8)),
            w_down=np.ascontiguousarray(np.asarray(w_down, np.int8)),
            s_gate=np.ascontiguousarray(np.asarray(s_gate, np.float32)),
            s_up=np.ascontiguousarray(np.asarray(s_up, np.float32)),
            s_down=np.ascontiguousarray(np.asarray(s_down, np.float32)),
            rope_c=cos, rope_s=sin,
            rope_co=np.ascontiguousarray(cos[rows]),
            rope_so=np.ascontiguousarray(sin[rows]),
            dmask=_diag_mask(r).astype(ml_dtypes.bfloat16),
        ))
    return in_maps


def kernel(L0_attn_x, L0_attn_w_qkv, L0_attn_w_proj, L0_attn_q_gain,
           L0_ffn_w_gate, L0_ffn_w_up, L0_ffn_w_down,
           L0_ffn_s_gate, L0_ffn_s_up, L0_ffn_s_down):
    x = np.ascontiguousarray(np.asarray(L0_attn_x, dtype=np.float32))
    nc = build_program("full")
    in_maps = make_in_maps(x, L0_attn_w_qkv, L0_attn_w_proj, L0_attn_q_gain,
                           L0_ffn_w_gate, L0_ffn_w_up, L0_ffn_w_down,
                           L0_ffn_s_gate, L0_ffn_s_up, L0_ffn_s_down)
    res = run_bass_kernel_spmd(nc, in_maps, core_ids=list(range(8)))
    out = np.empty((B, S, D), dtype=np.float32)
    for c in range(8):
        b, r = c // 4, c % 4
        out[b][_own_rows(r)] = res.results[c]["out_own"]
    return out
